# version 01
# baseline (speedup 1.0000x reference)
import numpy as np
import ml_dtypes
from collections import defaultdict
import concourse.bass as bass
import concourse.mybir as mybir
from concourse.bass_utils import run_bass_kernel_spmd

F32 = mybir.dt.float32
BF16 = mybir.dt.bfloat16
AF = mybir.ActivationFunctionType
ALU = mybir.AluOpType
AX = mybir.AxisListType

S = 2048
D = 1024
NCH = D // 128
NT = S // 128
NTG = S // 512
H = 16
HD = 64
NEG = -1.0e30

ENGINES = ("pe", "act", "dve", "pool", "sp")
DMAQ = ("sp", "pool", "act", "dve")


class Buf:
    __slots__ = ("name", "w", "r")

    def __init__(self, name):
        self.name = name
        self.w = []
        self.r = []


class Prog:
    def __init__(self, strict=False):
        self.plan = {e: [] for e in ENGINES}
        self.cnt = defaultdict(int)
        self.mark = {e: defaultdict(int) for e in ENGINES}
        self.strict = strict
        self.allbuf = Buf("__all__")
        self.dmasems = set()

    def _waits(self, eng, reads, writes):
        need = {}
        for b in reads:
            for (s, v) in b.w:
                if need.get(s, 0) < v:
                    need[s] = v
        for b in writes:
            for (s, v) in b.w + b.r:
                if need.get(s, 0) < v:
                    need[s] = v
        out = []
        for s, v in sorted(need.items()):
            if s.startswith("D"):
                v = self.cnt[s]
            if self.mark[eng][s] < v:
                out.append((s, v))
                self.mark[eng][s] = v
        return out

    def _record(self, tok, reads, writes):
        for b in writes:
            b.w = [tok]
            b.r = []
        for b in reads:
            b.r.append(tok)

    def op(self, eng, fn, reads=(), writes=()):
        if self.strict:
            writes = tuple(writes) + (self.allbuf,)
        waits = self._waits(eng, reads, writes)
        self.cnt[eng] += 1
        tok = (eng, self.cnt[eng])
        self.plan[eng].append((waits, fn, eng, 1))
        self._record(tok, reads, writes)
        return tok

    def group(self, eng, fns, reads=(), writes=()):
        if self.strict:
            writes = tuple(writes) + (self.allbuf,)
        waits = self._waits(eng, reads, writes)
        self.cnt[eng] += 1
        tok = (eng, self.cnt[eng])
        for i, fn in enumerate(fns):
            last = i == len(fns) - 1
            self.plan[eng].append((waits if i == 0 else [], fn,
                                   eng if last else None, 1))
        self._record(tok, reads, writes)
        return tok

    def dma(self, q, out_ap, in_ap, reads=(), writes=(), stream="", **kw):
        if self.strict:
            writes = tuple(writes) + (self.allbuf,)
        waits = self._waits(q, reads, writes)
        sem = "D" + q + (("_" + stream) if stream else "")
        self.dmasems.add(sem)
        self.cnt[sem] += 16
        tok = (sem, self.cnt[sem])
        self.plan[q].append((waits, ("dma", out_ap, in_ap, kw), sem, 16))
        self._record(tok, reads, writes)
        return tok

    def drain(self, eng, bufs):
        waits = self._waits(eng, (), bufs)
        self.plan[eng].append((waits, None, None, 0))

    def replay(self, nc):
        sems = {}
        for e in ENGINES:
            sems[e] = nc.alloc_semaphore("s_" + e)
        for q in self.dmasems:
            sems[q] = nc.alloc_semaphore("s" + q)
        plan = self.plan

        cnt = self.cnt

        def body(ename):
            def run(eng):
                for waits, fn, incsem, incamt in plan[ename]:
                    for (s, v) in waits:
                        eng.wait_ge(sems[s], v)
                    if fn is None:
                        eng.nop()
                        continue
                    if isinstance(fn, tuple):
                        ins = eng.dma_start(fn[1], fn[2], **fn[3])
                    else:
                        ins = fn(eng)
                    if incsem is not None:
                        ins.then_inc(sems[incsem], incamt)
                if ename == "sp":
                    for q in sorted(self.dmasems):
                        if cnt[q] > 0:
                            eng.wait_ge(sems[q], cnt[q])
            return run

        with nc.Block() as blk:
            blk.tensor(body("pe"))
            blk.scalar(body("act"))
            blk.vector(body("dve"))
            blk.gpsimd(body("pool"))
            blk.sync(body("sp"))



def host_constants():
    c = {}
    c["ident_f32"] = np.eye(128, dtype=np.float32)
    c["ident_bf16"] = np.eye(128, dtype=ml_dtypes.bfloat16)
    c["ones_col"] = np.ones((128, 1), dtype=np.float32)
    c["ones_row_f32"] = np.ones((1, 128), dtype=np.float32)
    c["ones_row_bf16"] = np.ones((1, 512), dtype=ml_dtypes.bfloat16)
    bs = 128
    q_off = np.arange(bs)
    k_off = np.arange(3 * bs)
    rel = k_off[None, :] - bs - q_off[:, None]
    c["mask_mid"] = np.where(np.abs(rel) <= bs, 0.0, NEG).astype(np.float32)
    k_off2 = np.arange(2 * bs)
    rel_f = k_off2[None, :] - q_off[:, None]
    c["mask_first"] = np.where(np.abs(rel_f) <= bs, 0.0, NEG).astype(np.float32)
    rel_l = k_off2[None, :] - bs - q_off[:, None]
    c["mask_last"] = np.where(np.abs(rel_l) <= bs, 0.0, NEG).astype(np.float32)
    tri = np.tril(np.ones((128, 128), dtype=bool))
    c["mask_tril"] = np.where(tri, 0.0, NEG).astype(ml_dtypes.bfloat16)
    stri = np.tril(np.ones((128, 128), dtype=bool), k=-1)
    c["mask_stril_f32"] = stri.astype(ml_dtypes.bfloat16)
    return c


def declare_constants(nc, c):
    aps = {}
    for k, v in c.items():
        dt = BF16 if v.dtype == ml_dtypes.bfloat16 else F32
        aps[k] = nc.dram_tensor("c_" + k, list(v.shape), dt, kind="ExternalInput").ap()
    return aps


STRICT = [False]
DBG = {}


class Ctx:
    def __init__(self, nc, const_aps):
        self.nc = nc
        self.p = Prog(strict=STRICT[0])
        self.caps = const_aps
        self.bufs = {}

    def sb(self, name, shape, dtype):
        t = self.nc.alloc_sbuf_tensor(name, shape, dtype)
        b = Buf(name)
        self.bufs[name] = (t, b)
        return t, b

    def ps(self, name, shape, dtype):
        t = self.nc.alloc_psum_tensor(name, shape, dtype)
        b = Buf(name)
        self.bufs[name] = (t, b)
        return t, b

    def load_const(self, key):
        name = "k_" + key
        if name in self.bufs:
            return self.bufs[name]
        ap = self.caps[key]
        t = self.nc.alloc_sbuf_tensor(name, list(ap.shape), ap.dtype)
        b = Buf(name)
        self.bufs[name] = (t, b)
        self.p.dma("sp", t[:, :], ap[:, :], writes=(b,), stream="c_" + key)
        return t, b

    def preload(self):
        for key in self.caps:
            self.load_const(key)

    def constcell(self, value):
        name = f"kc_{value!r}"
        if name in self.bufs:
            return self.bufs[name]
        t = self.nc.alloc_sbuf_tensor(name.replace('-', 'm').replace('.', '_'),
                                      [128, 1], F32)
        b = Buf(name)
        self.bufs[name] = (t, b)
        self.p.op("dve", (lambda t=t, value=value: (
            lambda e: e.memset(t[:, :], float(value))))(), writes=(b,))
        return t, b



def emit_load_x(ctx, x_dram_seq, X, bX, stage, bstage, psT32, bpsT32):
    p = ctx.p
    ident, bident = ctx.load_const("ident_f32")
    for tt in range(NT):
        p.dma("sp", stage[:, :], x_dram_seq[tt * 128:(tt + 1) * 128, :],
              writes=(bstage,), stream="io")
        for c in range(NCH):
            p.op("pe", (lambda c=c: (
                lambda e: e.transpose(psT32[:, 0:128], stage[:, c * 128:(c + 1) * 128], ident[:, :])))(),
                reads=(bstage, bident), writes=(bpsT32,))
            p.op("dve", (lambda c=c, tt=tt: (
                lambda e: e.tensor_copy(X[:, c * S + tt * 128: c * S + (tt + 1) * 128], psT32[:, 0:128])))(),
                reads=(bpsT32,), writes=(bX,))


def emit_store_x(ctx, X, bX, y_dram_seq, stage, bstage, psT32, bpsT32):
    p = ctx.p
    ident, bident = ctx.load_const("ident_f32")
    for tt in range(NT):
        for c in range(NCH):
            p.op("pe", (lambda c=c, tt=tt: (
                lambda e: e.transpose(psT32[:, 0:128], X[:, c * S + tt * 128: c * S + (tt + 1) * 128], ident[:, :])))(),
                reads=(bX, bident), writes=(bpsT32,))
            p.op("dve", (lambda c=c: (
                lambda e: e.tensor_copy(stage[:, c * 128:(c + 1) * 128], psT32[:, 0:128])))(),
                reads=(bpsT32,), writes=(bstage,))
        p.dma("sp", y_dram_seq[tt * 128:(tt + 1) * 128, :], stage[:, :],
              reads=(bstage,), writes=(), stream="io")


def emit_ln(ctx, B, eps=1e-5):
    p = ctx.p
    X, bX = B["X"]; XN, bXN = B["XN"]
    onescol, bonescol = ctx.load_const("ones_col")
    onesrow, bonesrow = ctx.load_const("ones_row_f32")
    ps1, bps1 = B["psB"]
    ps2, bps2 = B["psC"]
    pmb, bpmb = B["psA"]
    prb, bprb = B["psS"]
    sq, bsq = B["sq"]
    rows, brows = B["rows"]
    t1, bt1 = B["t1"]
    for tg in range(NTG):
        t0 = tg * 512
        fns1 = [(lambda c=c, t0=t0: (
            lambda e: e.matmul(ps1[0:1, :], onescol[:, :], X[:, c * S + t0: c * S + t0 + 512],
                               start=(c == 0), stop=(c == NCH - 1))))() for c in range(NCH)]
        p.group("pe", fns1, reads=(bX, bonescol), writes=(bps1,))
        for c in range(NCH):
            p.op("act", (lambda c=c, t0=t0: (
                lambda e: e.activation(sq[:, :], X[:, c * S + t0: c * S + t0 + 512], AF.Square)))(),
                reads=(bX,), writes=(bsq,))
            p.group("pe", [(lambda c=c: (
                lambda e: e.matmul(ps2[0:1, :], onescol[:, :], sq[:, :],
                                   start=(c == 0), stop=(c == NCH - 1))))()],
                reads=(bsq, bonescol), writes=(bps2,))
        m = rows[0:1, 0:512]; v = rows[0:1, 512:1024]
        sr = rows[0:1, 1024:1536]; r = rows[0:1, 1536:2048]
        p.op("dve", (lambda m=m: (lambda e: e.tensor_scalar_mul(m, ps1[0:1, :], 1.0 / D)))(),
             reads=(bps1,), writes=(brows,))
        p.op("dve", (lambda v=v: (lambda e: e.tensor_scalar_mul(v, ps2[0:1, :], 1.0 / D)))(),
             reads=(bps2,), writes=(brows,))
        p.op("dve", (lambda sr=sr, m=m: (lambda e: e.tensor_mul(sr, m, m)))(),
             reads=(brows,), writes=(brows,))
        p.op("dve", (lambda v=v, sr=sr: (lambda e: e.tensor_sub(v, v, sr)))(),
             reads=(brows,), writes=(brows,))
        epsc, bepsc = ctx.constcell(eps)
        p.op("act", (lambda sr=sr, v=v: (lambda e: e.activation(sr, v, AF.Sqrt, bias=epsc[0:1, 0:1])))(),
             reads=(brows, bepsc), writes=(brows,))
        p.op("dve", (lambda r=r, sr=sr: (lambda e: e.reciprocal(r, sr)))(),
             reads=(brows,), writes=(brows,))
        p.group("pe", [(lambda m=m: (
            lambda e: e.matmul(pmb[:, 0:512], onesrow[:, :], m, start=True, stop=True)))()],
            reads=(brows, bonesrow), writes=(bpmb,))
        p.group("pe", [(lambda r=r: (
            lambda e: e.matmul(prb[:, 0:512], onesrow[:, :], r, start=True, stop=True)))()],
            reads=(brows, bonesrow), writes=(bprb,))
        for c in range(NCH):
            p.op("dve", (lambda c=c, t0=t0: (
                lambda e: e.tensor_sub(t1[:, :], X[:, c * S + t0: c * S + t0 + 512], pmb[:, 0:512])))(),
                reads=(bX, bpmb), writes=(bt1,))
            p.op("dve", (lambda c=c, t0=t0: (
                lambda e: e.tensor_mul(XN[:, c * S + t0: c * S + t0 + 512], t1[:, :], prb[:, 0:512])))(),
                reads=(bt1, bprb), writes=(bXN,))


def emit_wt_granule_T(ctx, w_dram, oc, B, in_dim=D, q="pool"):
    p = ctx.p
    wg, bwg = B["wg"]; stage, bstage = B["wstage"]
    psT32, bpsT32 = B["psT32"]
    ident, bident = ctx.load_const("ident_f32")
    nic = in_dim // 128
    p.dma(q, stage[:, 0:in_dim], w_dram[oc * 128:(oc + 1) * 128, 0:in_dim],
          writes=(bstage,), stream="wgt")
    for ic in range(nic):
        p.op("pe", (lambda ic=ic: (
            lambda e: e.transpose(psT32[:, 0:128], stage[:, ic * 128:(ic + 1) * 128], ident[:, :])))(),
            reads=(bstage, bident), writes=(bpsT32,))
        p.op("dve", (lambda ic=ic: (
            lambda e: e.tensor_copy(wg[:, ic * 128:(ic + 1) * 128], psT32[:, 0:128])))(),
            reads=(bpsT32,), writes=(bwg,))
    return wg, bwg


def emit_w_granule(ctx, w_dram, oc, B, in_dim=D, out_w=128, q="pool",
                   bufkey="wg"):
    p = ctx.p
    wg, bwg = B[bufkey]
    stage, bstage = B["wstage" if bufkey == "wg" else "wstage2"]
    nic = in_dim // 128
    stage_cols = stage.shape[1]
    per = stage_cols // out_w
    srcfull = w_dram.rearrange("(a p) o -> p a o", p=128)
    strm = "wgt" if bufkey == "wg" else "wgt2"
    for a0 in range(0, nic, per):
        an = min(per, nic - a0)
        srcap = srcfull[:, a0:a0 + an, oc * out_w:(oc + 1) * out_w]
        p.dma(q, stage.rearrange("p (a o) -> p a o", o=out_w)[:, 0:an, :], srcap,
              writes=(bstage,), stream=strm)
        p.op("act", (lambda a0=a0, an=an, wg=wg, stage=stage, n=an * out_w: (
            lambda e: e.activation(wg[:, a0 * out_w: a0 * out_w + n], stage[:, 0:n], AF.Copy)))(),
            reads=(bstage,), writes=(bwg,))
    return wg, bwg


def emit_bias_row(ctx, v_dram, idx, n, B, q="pool"):
    p = ctx.p
    brow, bbrow = B["brow"]; stage, bstage = B["rowstage"]
    p.dma(q, stage[0:1, 0:n], v_dram.rearrange("(o n) -> o n", o=1), writes=(bstage,), stream="bias")
    p.op("dve", (lambda idx=idx, n=n: (
        lambda e: e.tensor_copy(brow[0:1, idx * D: idx * D + n], stage[0:1, 0:n])))(),
        reads=(bstage,), writes=(bbrow,))


def softmax_tile(ctx, ps_in, bps_in, Wd, mask_ap, bmask, scale, out_bf, bout, B):
    p = ctx.p
    sS, bsS = B["smS"]
    sE, bsE = B["smE"]
    rows, brows = B["smrows"]
    if mask_ap is not None:
        p.op("dve", (lambda: (
            lambda e: e.tensor_add(sS[:, 0:Wd], ps_in[:, 0:Wd], mask_ap)))(),
            reads=(bps_in, bmask), writes=(bsS,))
        src, bsrc = sS, bsS
    else:
        src, bsrc = ps_in, bps_in
    p.op("dve", (lambda: (
        lambda e: e.reduce_max(rows[:, 0:1], src[:, 0:Wd], AX.X, negate=True)))(),
        reads=(bsrc,), writes=(brows,))
    p.op("dve", (lambda: (
        lambda e: e.tensor_scalar_mul(rows[:, 1:2], rows[:, 0:1], float(scale))))(),
        reads=(brows,), writes=(brows,))
    p.op("act", (lambda: (
        lambda e: e.activation(sE[:, 0:Wd], src[:, 0:Wd], AF.Exp,
                               bias=rows[:, 1:2], scale=float(scale))))(),
        reads=(bsrc, brows), writes=(bsE,))
    p.op("dve", (lambda: (
        lambda e: e.reduce_sum(rows[:, 2:3], sE[:, 0:Wd], AX.X)))(),
        reads=(bsE,), writes=(brows,))
    p.op("dve", (lambda: (
        lambda e: e.reciprocal(rows[:, 3:4], rows[:, 2:3])))(),
        reads=(brows,), writes=(brows,))
    p.op("dve", (lambda: (
        lambda e: e.tensor_scalar_mul(out_bf[:, 0:Wd], sE[:, 0:Wd], rows[:, 3:4])))(),
        reads=(bsE, brows), writes=(bout,))


def emit_out_proj_residual(ctx, B, w_dram, bias_idx, src_key="AO", transposed=True):
    p = ctx.p
    X, bX = B["X"]
    AO, bAO = B[src_key]
    psA, bpsA = B["psA"]
    brow, bbrow = B.get("brow", (None, None))
    onesrow_b, bonesrow_b = ctx.load_const("ones_row_bf16")
    for oc in range(NCH):
        if transposed:
            wg, bwg = emit_wt_granule_T(ctx, w_dram, oc, B)
        else:
            wg, bwg = emit_w_granule(ctx, w_dram, oc, B)
        for tg in range(NTG):
            t0 = tg * 512
            fns = []
            for ic in range(NCH):
                fns.append((lambda ic=ic, t0=t0, last=(bias_idx is None and ic == NCH - 1): (
                    lambda e: e.matmul(psA[:, :], wg[:, ic * 128:(ic + 1) * 128],
                                       AO[:, ic * S + t0: ic * S + t0 + 512],
                                       start=(ic == 0), stop=last)))())
            if bias_idx is not None:
                fns.append((lambda oc=oc, bias_idx=bias_idx: (
                    lambda e: e.matmul(psA[:, :], brow[0:1, bias_idx * D + oc * 128: bias_idx * D + (oc + 1) * 128],
                                       onesrow_b[0:1, :], start=False, stop=True)))())
            rds = (bwg, bAO, bonesrow_b) + ((bbrow,) if bbrow is not None else ())
            p.group("pe", fns, reads=rds, writes=(bpsA,))
            p.op("dve", (lambda oc=oc, t0=t0: (
                lambda e: e.tensor_add(X[:, oc * S + t0: oc * S + t0 + 512],
                                       X[:, oc * S + t0: oc * S + t0 + 512], psA[:, :])))(),
                reads=(bpsA,), writes=(bX,))



def emit_l0_attn_a(ctx, B, W):
    p = ctx.p
    X, bX = B["X"]; XN, bXN = B["XN"]
    qp, bqp = B["qp"]; kp, bkp = B["kp"]; vtp, bvtp = B["vtp"]
    vstage, bvstage = B["vstage"]
    AO, bAO = B["AO"]
    PT, bPT = B["PT"]; Pb, bPb = B["Pbuf"]
    psS, bpsS = B["psS"]; psA, bpsA = B["psA"]
    psB, bpsB = B["psB"]; psC, bpsC = B["psC"]
    psTb, bpsTb = B["psTbf"]
    brow, bbrow = B["brow"]
    identb, bidentb = ctx.load_const("ident_bf16")
    onesrow_b, bonesrow_b = ctx.load_const("ones_row_bf16")
    masks = {mk: ctx.load_const(mk) for mk in ("mask_first", "mask_mid", "mask_last")}

    for i, bn in enumerate(("bq", "bk", "bv", "bo")):
        emit_bias_row(ctx, W[bn], i, D, B)

    scale = 1.0 / np.sqrt(HD)
    for c in range(NCH):
        for wkey, dstinfo in (("Wq", ("qp", False)), ("Wk", ("kp", False)),
                              ("Wv", (None, True))):
            wg, bwg = emit_wt_granule_T(ctx, W[wkey], c, B)
            bias_idx = {"Wq": 0, "Wk": 1, "Wv": 2}[wkey]
            for tg in range(NTG):
                t0 = tg * 512
                fns = []
                for ic in range(NCH):
                    fns.append((lambda ic=ic, t0=t0: (
                        lambda e: e.matmul(psB[:, :], wg[:, ic * 128:(ic + 1) * 128],
                                           XN[:, ic * S + t0: ic * S + t0 + 512],
                                           start=(ic == 0), stop=False)))())
                fns.append((lambda c=c, bias_idx=bias_idx: (
                    lambda e: e.matmul(psB[:, :], brow[0:1, bias_idx * D + c * 128: bias_idx * D + (c + 1) * 128],
                                       onesrow_b[0:1, :], start=False, stop=True)))())
                p.group("pe", fns, reads=(bwg, bXN, bbrow, bonesrow_b), writes=(bpsB,))
                if not dstinfo[1]:
                    dst, bdst = B[dstinfo[0]]
                    p.op("dve", (lambda dst=dst, t0=t0: (
                        lambda e: e.tensor_copy(dst[:, t0:t0 + 512], psB[:, :])))(),
                        reads=(bpsB,), writes=(B[dstinfo[0]][1],))
                else:
                    p.op("dve", (lambda t0=t0: (
                        lambda e: e.tensor_copy(vstage[:, :], psB[:, :])))(),
                        reads=(bpsB,), writes=(bvstage,))
                    for j in range(4):
                        tt = tg * 4 + j
                        p.op("pe", (lambda j=j: (
                            lambda e: e.transpose(psTb[:, 0:128], vstage[:, j * 128:(j + 1) * 128], identb[:, :])))(),
                            reads=(bvstage, bidentb), writes=(bpsTb,))
                        p.op("dve", (lambda tt=tt: (
                            lambda e: e.tensor_copy(vtp[:, tt * 128:(tt + 1) * 128], psTb[:, 0:128])))(),
                            reads=(bpsTb,), writes=(bvtp,))
        for tt in range(NT):
            if tt == 0:
                k0, nkt, mkey = 0, 2, "mask_first"
            elif tt == NT - 1:
                k0, nkt, mkey = tt - 1, 2, "mask_last"
            else:
                k0, nkt, mkey = tt - 1, 3, "mask_mid"
            Wk_ = nkt * 128
            mten, mbuf = masks[mkey]
            for hh in range(2):
                rb = hh * 64
                p.group("pe", [(lambda rb=rb, tt=tt, k0=k0, Wk_=Wk_: (
                    lambda e: e.matmul(psS[:, 0:Wk_], qp[rb:rb + 64, tt * 128:(tt + 1) * 128],
                                       kp[rb:rb + 64, k0 * 128: k0 * 128 + Wk_],
                                       start=True, stop=True)))()],
                    reads=(bqp, bkp), writes=(bpsS,))
                softmax_tile(ctx, psS, bpsS, Wk_, mten[:, 0:Wk_], mbuf, scale, Pb, bPb, B)
                for j in range(nkt):
                    p.op("pe", (lambda j=j: (
                        lambda e: e.transpose(psTb[:, j * 128:(j + 1) * 128],
                                              Pb[:, j * 128:(j + 1) * 128], identb[:, :])))(),
                        reads=(bPb, bidentb), writes=(bpsTb,))
                p.op("dve", (lambda n=nkt * 128: (
                    lambda e: e.tensor_copy(PT[:, 0:n], psTb[:, 0:n])))(),
                    reads=(bpsTb,), writes=(bPT,))
                fns = []
                for j in range(nkt):
                    kt = k0 + j
                    fns.append((lambda kt=kt, rb=rb, j=j, first=(j == 0), last=(j == nkt - 1): (
                        lambda e: e.matmul(psC[0:64, 0:128], vtp[:, kt * 128 + rb: kt * 128 + rb + 64],
                                           PT[:, j * 128:(j + 1) * 128], start=first, stop=last)))())
                p.group("pe", fns, reads=(bvtp, bPT), writes=(bpsC,))
                p.op("dve", (lambda c=c, rb=rb, tt=tt: (
                    lambda e: e.tensor_copy(AO[rb:rb + 64, c * S + tt * 128: c * S + (tt + 1) * 128],
                                            psC[0:64, 0:128])))(),
                    reads=(bpsC,), writes=(bAO,))

    emit_out_proj_residual(ctx, B, W["Wo"], 3, "AO", transposed=True)



def alloc_common(ctx, smw=2048, smew=2048, need_brow=True, smedt=F32):
    B = {}
    B["X"] = ctx.sb("X", [128, NCH * S], F32)
    B["XN"] = ctx.sb("XN", [128, NCH * S], BF16)
    B["stage"] = ctx.sb("stage", [128, D], F32)
    B["t1"] = ctx.sb("t1", [128, 512], F32)
    B["sq"] = ctx.sb("sq", [128, 512], F32)
    B["rows"] = ctx.sb("rows", [1, 2048], F32)
    B["smS"] = ctx.sb("smS", [128, smw], F32)
    B["smE"] = ctx.sb("smE", [128, smew], smedt)
    B["smrows"] = ctx.sb("smrows", [128, 8], F32)
    B["wg"] = ctx.sb("wgran", [128, NCH * 128], BF16)
    B["wstage"] = B["stage"]
    if need_brow:
        B["brow"] = ctx.sb("brow", [1, 4 * D], BF16)
        if need_brow == "slim":
            B["rowstage"] = B["rows"]
        else:
            B["rowstage"] = ctx.sb("rowstage", [1, D], F32)
    B["psS"] = ctx.ps("psS", [128, 2048], F32)
    B["psA"] = ctx.ps("psA", [128, 512], F32)
    B["psB"] = ctx.ps("psB", [128, 512], F32)
    B["psC"] = ctx.ps("psC", [128, 512], F32)
    B["psTbf"] = ctx.ps("psTbf", [128, 512], BF16)
    B["psT32"] = (B["psC"][0], B["psC"][1])
    return B


def build_l0_attn_a_program():
    consts = host_constants()
    nc = bass.Bass("TRN2", target_bir_lowering=False)
    caps = declare_constants(nc, consts)
    x_in = nc.dram_tensor("x_in", [2, S, D], F32, kind="ExternalInput").ap()
    x_out = nc.dram_tensor("x_out", [2, S, D], F32, kind="ExternalOutput").ap()
    Wd = {}
    for nm, shape in (("Wq", [D, D]), ("bq", [D]), ("Wk", [D, D]), ("bk", [D]),
                      ("Wv", [D, D]), ("bv", [D]), ("Wo", [D, D]), ("bo", [D])):
        Wd[nm] = nc.dram_tensor("w_" + nm, shape, F32, kind="ExternalInput").ap()
    ctx = Ctx(nc, caps)
    B = alloc_common(ctx, smw=512, smew=512)
    B["qp"] = ctx.sb("qp", [128, S], BF16)
    B["kp"] = ctx.sb("kp", [128, S], BF16)
    B["vtp"] = ctx.sb("vtp", [128, NT * 128], BF16)
    B["vstage"] = ctx.sb("vstage", [128, 512], BF16)
    B["AO"] = ctx.sb("AO", [128, NCH * S], BF16)
    B["PT"] = ctx.sb("PTb", [128, 3 * 128], BF16)
    B["Pbuf"] = ctx.sb("Pbuf", [128, 384], BF16)
    dbg_aps = {}
    if DBG.get("on"):
        for nm, shape, dt in (("dX", [128, NCH * S], F32), ("dXN", [128, NCH * S], BF16),
                              ("dqp", [128, S], BF16), ("dkp", [128, S], BF16),
                              ("dvtp", [128, NT * 128], BF16), ("dAO", [128, NCH * S], BF16)):
            dbg_aps[nm] = nc.dram_tensor(nm, shape, dt, kind="ExternalOutput").ap()
    for seq in range(2):
        emit_load_x(ctx, x_in[seq], B["X"][0], B["X"][1], B["stage"][0], B["stage"][1],
                    B["psT32"][0], B["psT32"][1])
        if seq == 0 and dbg_aps:
            ctx.p.dma("pool", dbg_aps["dX"][:, :], B["X"][0][:, :], reads=(B["X"][1],), stream="dbg")
        emit_ln(ctx, B)
        if seq == 0 and dbg_aps:
            ctx.p.dma("pool", dbg_aps["dXN"][:, :], B["XN"][0][:, :], reads=(B["XN"][1],), stream="dbg")
        emit_l0_attn_a(ctx, B, Wd)
        if seq == 0 and dbg_aps:
            for nm, key in (("dqp", "qp"), ("dkp", "kp"), ("dvtp", "vtp"), ("dAO", "AO")):
                ctx.p.dma("pool", dbg_aps[nm][:, :], B[key][0][:, :], reads=(B[key][1],), stream="dbg")
        emit_store_x(ctx, B["X"][0], B["X"][1], x_out[seq], B["stage"][0], B["stage"][1],
                     B["psT32"][0], B["psT32"][1])
    ctx.p.drain("sp", [B["stage"][1]])
    ctx.p.replay(nc)
    return nc, consts



def emit_attn_core(ctx, B, qtile_ap, k_ap, bk, vtile, bv, write_out, scale,
                   causal, use_softmax=True, bq_src=None):
    p = ctx.p
    Pb, bPb = B["Pbuf"]
    PT, bPT = B["PT"]
    psS, bpsS = B["psS"]
    psA, bpsA = B["psA"]
    psTb, bpsTb = B["psTbf"]
    identb, bidentb = ctx.load_const("ident_bf16")
    if causal and use_softmax:
        mtril, bmtril = ctx.load_const("mask_tril")
    if not use_softmax:
        mstril, bmstril = ctx.load_const("mask_stril_f32")
    for qg in range(4):
        ktmax = 4 * qg + 4 if causal else NT
        if causal:
            p.op("dve", (lambda n=ktmax * 512: (
                lambda e: e.memset(PT[:, 0:n], 0.0)))(), writes=(bPT,))
        for sub in range(4):
            tt = qg * 4 + sub
            Wd = (tt + 1) * 128 if causal else S
            fns = []
            for kg0 in range(0, Wd, 512):
                kw = min(512, Wd - kg0)
                fns.append((lambda tt=tt, kg0=kg0, kw=kw: (
                    lambda e: e.matmul(psS[:, kg0:kg0 + kw], qtile_ap(tt),
                                       k_ap[:, kg0:kg0 + kw],
                                       start=True, stop=True)))())
            p.group("pe", fns, reads=(bq_src, bk), writes=(bpsS,))
            nkt = Wd // 128
            if use_softmax:
                if causal:
                    p.op("dve", (lambda Wd=Wd: (
                        lambda e: e.tensor_add(psS[:, Wd - 128:Wd], psS[:, Wd - 128:Wd],
                                               mtril[:, :])))(),
                        reads=(bmtril,), writes=(bpsS,))
                softmax_tile(ctx, psS, bpsS, Wd, None, None, scale, Pb, bPb, B)
            else:
                if Wd > 128:
                    p.op("dve", (lambda Wd=Wd: (
                        lambda e: e.tensor_scalar_mul(Pb[:, 0:Wd - 128], psS[:, 0:Wd - 128],
                                                      float(scale))))(),
                        reads=(bpsS,), writes=(bPb,))
                p.op("dve", (lambda Wd=Wd: (
                    lambda e: e.scalar_tensor_tensor(Pb[:, Wd - 128:Wd], psS[:, Wd - 128:Wd],
                                                     float(scale), mstril[:, :],
                                                     ALU.mult, ALU.mult)))(),
                    reads=(bpsS, bmstril), writes=(bPb,))
            for j0 in range(0, nkt, 4):
                jn = min(4, nkt - j0)
                for j in range(jn):
                    p.op("pe", (lambda j0=j0, j=j: (
                        lambda e: e.transpose(psTb[:, j * 128:(j + 1) * 128],
                                              Pb[:, (j0 + j) * 128:(j0 + j + 1) * 128],
                                              identb[:, :])))(),
                        reads=(bPb, bidentb), writes=(bpsTb,))
                for j in range(jn):
                    p.op("dve", (lambda j0=j0, j=j, sub=sub: (
                        lambda e: e.tensor_copy(PT[:, (j0 + j) * 512 + sub * 128:
                                                   (j0 + j) * 512 + sub * 128 + 128],
                                                psTb[:, j * 128:(j + 1) * 128])))(),
                        reads=(bpsTb,), writes=(bPT,))
        fns = []
        for kt in range(ktmax):
            fns.append((lambda kt=kt, first=(kt == 0), last=(kt == ktmax - 1): (
                lambda e: e.matmul(psA[0:64, :], vtile(kt), PT[:, kt * 512:(kt + 1) * 512],
                                   start=first, stop=last)))())
        p.group("pe", fns, reads=(bv, bPT), writes=(bpsA,))
        write_out(qg, psA, bpsA)


def emit_pair_proj(ctx, B, w_dram, dst_pair, oc, src_pair, in_dim=D,
                   transposed=True, bias_slice=None, to_tok=None, dst_off=0,
                   evac=None):
    p = ctx.p
    src, bsrc = src_pair
    psB, bpsB = B["psB"]
    psTb, bpsTb = B["psTbf"]
    vstage, bvstage = B.get("vstage", (None, None))
    identb, bidentb = ctx.load_const("ident_bf16")
    onesrow_b, bonesrow_b = ctx.load_const("ones_row_bf16")
    if transposed:
        wg, bwg = emit_wt_granule_T(ctx, w_dram, oc, B, in_dim=in_dim)
    else:
        wg, bwg = emit_w_granule(ctx, w_dram, oc, B, in_dim=in_dim)
    nic = in_dim // 128
    brow_b = B["brow"][1] if "brow" in B else bwg
    for tg in range(NTG):
        t0 = tg * 512
        fns = []
        for ic in range(nic):
            fns.append((lambda ic=ic, t0=t0, last=(bias_slice is None and ic == nic - 1): (
                lambda e: e.matmul(psB[:, :], wg[:, ic * 128:(ic + 1) * 128],
                                   src[:, ic * S + t0: ic * S + t0 + 512],
                                   start=(ic == 0), stop=last)))())
        if bias_slice is not None:
            fns.append((lambda: (
                lambda e: e.matmul(psB[:, :], bias_slice, onesrow_b[0:1, :],
                                   start=False, stop=True)))())
        p.group("pe", fns, reads=(bwg, bsrc, brow_b, bonesrow_b), writes=(bpsB,))
        if dst_pair is not None:
            dst, bdst = dst_pair
            if evac is None:
                p.op("dve", (lambda t0=t0, dst_off=dst_off: (
                    lambda e: e.tensor_copy(dst[:, dst_off + t0: dst_off + t0 + 512], psB[:, :])))(),
                    reads=(bpsB,), writes=(bdst,))
            else:
                func, scl = evac
                p.op("act", (lambda t0=t0, dst_off=dst_off, func=func, scl=scl: (
                    lambda e: e.activation(dst[:, dst_off + t0: dst_off + t0 + 512], psB[:, :],
                                           func, scale=float(scl))))(),
                    reads=(bpsB,), writes=(bdst,))
        if to_tok is not None:
            vt, bvt = to_tok
            if dst_pair is not None:
                tsrc = dst[:, dst_off + t0: dst_off + t0 + 512]
                btsrc = dst_pair[1]
            else:
                p.op("dve", (lambda: (
                    lambda e: e.tensor_copy(vstage[:, :], psB[:, :])))(),
                    reads=(bpsB,), writes=(bvstage,))
                tsrc = vstage[:, 0:512]
                btsrc = bvstage
            for j in range(4):
                tt = tg * 4 + j
                p.op("pe", (lambda j=j, tsrc=tsrc: (
                    lambda e: e.transpose(psTb[:, 0:128], tsrc[:, j * 128:(j + 1) * 128],
                                          identb[:, :])))(),
                    reads=(btsrc, bidentb), writes=(bpsTb,))
                p.op("dve", (lambda tt=tt: (
                    lambda e: e.tensor_copy(vt[:, tt * 128:(tt + 1) * 128], psTb[:, 0:128])))(),
                    reads=(bpsTb,), writes=(bvt,))



def emit_l0_attn_b(ctx, B, W):
    p = ctx.p
    X, bX = B["X"]; XN, bXN = B["XN"]
    AO, bAO = B["AO"]
    lat, blat = B["lat"]
    qp, bqp = B["qp"]; kp, bkp = B["kp"]
    vtp, bvtp = B["vtp"]
    scale = 1.0 / np.sqrt(HD)
    wq2d = W["Wq"].rearrange("d h k -> d (h k)")
    wku = W["Wk_up"].rearrange("c h k -> c (h k)")
    wvu = W["Wv_up"].rearrange("c h k -> c (h k)")
    wout = W["Wout"].rearrange("h k o -> (h k) o")
    for oc in range(2):
        emit_pair_proj(ctx, B, W["Wkv_down"], (lat, blat), oc, (XN, bXN),
                       transposed=False, dst_off=oc * S)
    for c in range(NCH):
        emit_pair_proj(ctx, B, wq2d, (qp, bqp), c, (XN, bXN), transposed=False)
        emit_pair_proj(ctx, B, wku, (kp, bkp), c, (lat, blat), in_dim=256,
                       transposed=False)
        emit_pair_proj(ctx, B, wvu, None, c, (lat, blat), in_dim=256,
                       transposed=False, to_tok=(vtp, bvtp))
        for hh in range(2):
            rb = hh * 64
            qtile_ap = (lambda rb=rb: (lambda tt: qp[rb:rb + 64, tt * 128:(tt + 1) * 128]))()
            vtile = (lambda rb=rb: (lambda kt: vtp[:, kt * 128 + rb: kt * 128 + rb + 64]))()
            def write_out(qg, ps, bps, c=c, rb=rb):
                p.op("dve", (lambda qg=qg, c=c, rb=rb, ps=ps: (
                    lambda e: e.tensor_copy(AO[rb:rb + 64, c * S + qg * 512: c * S + qg * 512 + 512],
                                            ps[0:64, :])))(),
                    reads=(bps,), writes=(bAO,))
            emit_attn_core(ctx, B, qtile_ap, kp[rb:rb + 64, :], bkp, vtile, bvtp,
                           write_out, scale, causal=True, use_softmax=True,
                           bq_src=bqp)
    emit_out_proj_residual(ctx, B, wout, None, "AO", transposed=False)



def emit_ffn(ctx, B, W, kind):
    p = ctx.p
    X, bX = B["X"]; XN, bXN = B["XN"]
    hg, bhg = B["hg"]
    gates, bgates = B["gates"]
    GMT, bGMT = B["GMT"]
    grow, bgrow = B["grow"]
    oacc, boacc = B["oacc"]
    sq, bsq = B["sq"]; t1, bt1 = B["t1"]
    rows, brows = B["rows"]
    srows, bsrows = B["smrows"]
    psS, bpsS = B["psS"]; psA, bpsA = B["psA"]
    psB, bpsB = B["psB"]; psC, bpsC = B["psC"]
    gwT, bgwT = B["gwT"]
    stage, bstage = B["wstage"]
    ident, bident = ctx.load_const("ident_f32")
    onesrow, bonesrow = ctx.load_const("ones_row_f32")
    onescol, bonescol = ctx.load_const("ones_col")
    onesrow_b, bonesrow_b = ctx.load_const("ones_row_bf16")
    E = 8
    NH = 4096 // 128
    NEGBIG = -1.0e30

    if kind == "l1":
        rwf, brwf = B["rwf"]
        p.dma("pool", stage[0:NCH, 0:128], W["rms_w"].rearrange("(c p) -> c p", p=128),
              writes=(bstage,), stream="bias")
        p.op("pe", (lambda: (
            lambda e: e.transpose(psC[:, 0:NCH], stage[0:NCH, 0:128], ident[0:NCH, 0:NCH])))(),
            reads=(bstage, bident), writes=(bpsC,))
        p.op("dve", (lambda: (lambda e: e.tensor_copy(rwf[:, :], psC[:, 0:NCH])))(),
             reads=(bpsC,), writes=(brwf,))
        epsc, bepsc = ctx.constcell(1e-6)
        for tg in range(NTG):
            t0 = tg * 512
            for c in range(NCH):
                p.op("act", (lambda c=c, t0=t0: (
                    lambda e: e.activation(sq[:, :], XN[:, c * S + t0: c * S + t0 + 512], AF.Square)))(),
                    reads=(bXN,), writes=(bsq,))
                p.group("pe", [(lambda c=c: (
                    lambda e: e.matmul(psB[0:1, :], onescol[:, :], sq[:, :],
                                       start=(c == 0), stop=(c == NCH - 1))))()],
                    reads=(bsq, bonescol), writes=(bpsB,))
            ms = rows[0:1, 0:512]; sr = rows[0:1, 512:1024]; rr = rows[0:1, 1024:1536]
            p.op("dve", (lambda ms=ms: (lambda e: e.tensor_scalar_mul(ms, psB[0:1, :], 1.0 / D)))(),
                 reads=(bpsB,), writes=(brows,))
            p.op("act", (lambda sr=sr, ms=ms: (
                lambda e: e.activation(sr, ms, AF.Sqrt, bias=epsc[0:1, 0:1])))(),
                reads=(brows, bepsc), writes=(brows,))
            p.op("dve", (lambda rr=rr, sr=sr: (lambda e: e.reciprocal(rr, sr)))(),
                 reads=(brows,), writes=(brows,))
            p.group("pe", [(lambda rr=rr: (
                lambda e: e.matmul(psA[:, 0:512], onesrow[:, :], rr, start=True, stop=True)))()],
                reads=(brows, bonesrow), writes=(bpsA,))
            for c in range(NCH):
                p.op("dve", (lambda c=c, t0=t0: (
                    lambda e: e.tensor_mul(t1[:, :], XN[:, c * S + t0: c * S + t0 + 512], psA[:, 0:512])))(),
                    reads=(bXN, bpsA), writes=(bt1,))
                p.op("dve", (lambda c=c, t0=t0: (
                    lambda e: e.tensor_scalar_mul(XN[:, c * S + t0: c * S + t0 + 512], t1[:, :],
                                                  rwf[:, c:c + 1])))(),
                    reads=(bt1, brwf), writes=(bXN,))

    if kind == "l0":
        p.dma("pool", stage[:, 0:NCH * 8].rearrange("p (c o) -> p c o", o=8),
              W["router_w"].rearrange("(c p) o -> p c o", p=128),
              writes=(bstage,), stream="bias")
        p.op("dve", (lambda: (lambda e: e.tensor_copy(gwT[:, :], stage[:, 0:NCH * 8])))(),
             reads=(bstage,), writes=(bgwT,))
    else:
        p.dma("pool", stage[0:8, 0:D], W["gate_w"][:, :], writes=(bstage,), stream="bias")
        for c in range(NCH):
            p.op("pe", (lambda c=c: (
                lambda e: e.transpose(psC[:, 0:8], stage[0:8, c * 128:(c + 1) * 128],
                                      ident[0:8, 0:8])))(),
                reads=(bstage, bident), writes=(bpsC,))
            p.op("dve", (lambda c=c: (
                lambda e: e.tensor_copy(gwT[:, c * 8:(c + 1) * 8], psC[:, 0:8])))(),
                reads=(bpsC,), writes=(bgwT,))

    for tt in range(NT):
        fns = []
        for c in range(NCH):
            fns.append((lambda c=c, tt=tt: (
                lambda e: e.matmul(psC[:, 0:8], XN[:, c * S + tt * 128: c * S + (tt + 1) * 128],
                                   gwT[:, c * 8:(c + 1) * 8],
                                   start=(c == 0), stop=(c == NCH - 1))))())
        p.group("pe", fns, reads=(bXN, bgwT), writes=(bpsC,))
        g0 = tt * 8
        p.op("dve", (lambda: (
            lambda e: e.reduce_max(srows[:, 0:1], psC[:, 0:8], AX.X, negate=True)))(),
            reads=(bpsC,), writes=(bsrows,))
        p.op("act", (lambda: (
            lambda e: e.activation(t1[:, 0:8], psC[:, 0:8], AF.Exp, bias=srows[:, 0:1])))(),
            reads=(bpsC, bsrows), writes=(bt1,))
        p.op("dve", (lambda: (
            lambda e: e.reduce_sum(srows[:, 1:2], t1[:, 0:8], AX.X)))(),
            reads=(bt1,), writes=(bsrows,))
        p.op("dve", (lambda: (
            lambda e: e.reciprocal(srows[:, 2:3], srows[:, 1:2])))(),
            reads=(bsrows,), writes=(bsrows,))
        p.op("dve", (lambda: (
            lambda e: e.tensor_scalar_mul(t1[:, 16:24], t1[:, 0:8], srows[:, 2:3])))(),
            reads=(bt1, bsrows), writes=(bt1,))
        if kind == "l1":
            p.op("dve", (lambda: (
                lambda e: e.reduce_max(srows[:, 3:4], t1[:, 16:24], AX.X)))(),
                reads=(bt1,), writes=(bsrows,))
            p.op("dve", (lambda: (
                lambda e: e.tensor_single_scalar(t1[:, 24:32], t1[:, 16:24], srows[:, 3:4], ALU.is_ge)))(),
                reads=(bt1, bsrows), writes=(bt1,))
            p.op("dve", (lambda g0=g0: (
                lambda e: e.tensor_scalar_mul(gates[:, g0:g0 + 8], t1[:, 24:32], srows[:, 3:4])))(),
                reads=(bt1, bsrows), writes=(bgates,))
        else:
            p.op("dve", (lambda: (
                lambda e: e.tensor_copy(t1[:, 24:32], t1[:, 16:24])))(),
                reads=(bt1,), writes=(bt1,))
            for j in range(3):
                p.op("dve", (lambda: (
                    lambda e: e.reduce_max(srows[:, 3:4], t1[:, 24:32], AX.X)))(),
                    reads=(bt1,), writes=(bsrows,))
                p.op("dve", (lambda: (
                    lambda e: e.tensor_single_scalar(t1[:, 8:16], t1[:, 24:32], srows[:, 3:4], ALU.is_ge)))(),
                    reads=(bt1, bsrows), writes=(bt1,))
                p.op("dve", (lambda: (
                    lambda e: e.scalar_tensor_tensor(t1[:, 24:32], t1[:, 8:16], NEGBIG, t1[:, 24:32],
                                                     ALU.mult, ALU.add)))(),
                    reads=(bt1,), writes=(bt1,))
            p.op("dve", (lambda: (
                lambda e: e.reduce_max(srows[:, 3:4], t1[:, 24:32], AX.X)))(),
                reads=(bt1,), writes=(bsrows,))
            p.op("dve", (lambda: (
                lambda e: e.tensor_single_scalar(t1[:, 8:16], t1[:, 16:24], srows[:, 3:4], ALU.is_ge)))(),
                reads=(bt1, bsrows), writes=(bt1,))
            p.op("dve", (lambda: (
                lambda e: e.tensor_mul(t1[:, 24:32], t1[:, 16:24], t1[:, 8:16])))(),
                reads=(bt1,), writes=(bt1,))
            p.op("dve", (lambda: (
                lambda e: e.reduce_sum(srows[:, 4:5], t1[:, 24:32], AX.X)))(),
                reads=(bt1,), writes=(bsrows,))
            p.op("dve", (lambda: (
                lambda e: e.reciprocal(srows[:, 5:6], srows[:, 4:5])))(),
                reads=(bsrows,), writes=(bsrows,))
            p.op("dve", (lambda g0=g0: (
                lambda e: e.tensor_scalar_mul(gates[:, g0:g0 + 8], t1[:, 24:32], srows[:, 5:6])))(),
                reads=(bt1, bsrows), writes=(bgates,))
        p.op("pe", (lambda g0=g0: (
            lambda e: e.transpose(psC[0:8, 0:128], gates[:, g0:g0 + 8], ident[:, :])))(),
            reads=(bgates, bident), writes=(bpsC,))
        p.op("dve", (lambda tt=tt: (
            lambda e: e.tensor_copy(GMT[0:8, tt * 128:(tt + 1) * 128], psC[0:8, 0:128])))(),
            reads=(bpsC,), writes=(bGMT,))

    if kind == "l1":
        b1f, bb1f = B["b1f"]
        b2b, bb2b = B["b2b"]
        stage2, bstage2 = B["wstage2"]
        for e_ in range(E):
            p.dma("pool", stage[0:NH, 0:128],
                  W["b1"][e_].rearrange("(c p) -> c p", p=128), writes=(bstage,), stream="bias")
            p.op("pe", (lambda: (
                lambda e: e.transpose(psC[:, 0:NH], stage[0:NH, 0:128], ident[0:NH, 0:NH])))(),
                reads=(bstage, bident), writes=(bpsC,))
            p.op("dve", (lambda e_=e_: (
                lambda e: e.tensor_copy(b1f[:, e_ * NH:(e_ + 1) * NH], psC[:, 0:NH])))(),
                reads=(bpsC,), writes=(bb1f,))
        p.dma("pool", stage2[0:8, 0:D], W["b2"][:, :], writes=(bstage2,), stream="bias")
        p.op("dve", (lambda: (lambda e: e.tensor_copy(b2b[0:8, 0:D], stage2[0:8, 0:D])))(),
             reads=(bstage2,), writes=(bb2b,))

    w1s = [W["up_w"][e_] if kind == "l0" else W["W1"][e_] for e_ in range(E)]
    w2s = [W["down_w"][e_] if kind == "l0" else W["W2"][e_] for e_ in range(E)]
    for tg in range(NTG):
        t0 = tg * 512
        for dc in range(NCH):
            p.op("dve", (lambda dc=dc: (
                lambda e: e.memset(oacc[:, dc * 512:(dc + 1) * 512], 0.0)))(),
                writes=(boacc,))
        for e_ in range(E):
            p.dma("sp", grow[0:1, 0:S], GMT[0:1, 0:S].tensor[e_:e_ + 1, 0:S]
                  if False else GMT[e_:e_ + 1, 0:S],
                  reads=(bGMT,), writes=(bgrow,), stream="grow")
            p.group("pe", [(lambda t0=t0: (
                lambda e: e.matmul(psS[:, 1024:1536], onesrow_b[0:1, 0:128],
                                   grow[0:1, t0:t0 + 512], start=True, stop=True)))()],
                reads=(bgrow, bonesrow_b), writes=(bpsS,))
            for hc in range(NH):
                ping = hc % 2
                wg, bwg = emit_w_granule(ctx, w1s[e_], hc, B, in_dim=D)
                fns = []
                for ic in range(NCH):
                    fns.append((lambda ic=ic, t0=t0, ping=ping: (
                        lambda e: e.matmul(psS[:, ping * 512:(ping + 1) * 512],
                                           wg[:, ic * 128:(ic + 1) * 128],
                                           XN[:, ic * S + t0: ic * S + t0 + 512],
                                           start=(ic == 0), stop=(ic == NCH - 1))))())
                p.group("pe", fns, reads=(bwg, bXN), writes=(bpsS,))
                if kind == "l0":
                    p.op("act", (lambda ping=ping: (
                        lambda e: e.activation(t1[:, :], psS[:, ping * 512:(ping + 1) * 512],
                                               AF.Gelu_apprx_tanh)))(),
                        reads=(bpsS,), writes=(bt1,))
                else:
                    p.op("act", (lambda ping=ping, e_=e_, hc=hc: (
                        lambda e: e.activation(t1[:, :], psS[:, ping * 512:(ping + 1) * 512],
                                               AF.Silu,
                                               bias=B["b1f"][0][:, e_ * NH + hc: e_ * NH + hc + 1])))(),
                        reads=(bpsS, B["b1f"][1]), writes=(bt1,))
                p.op("dve", (lambda hc=hc: (
                    lambda e: e.tensor_mul(hg[:, hc * 512:(hc + 1) * 512], t1[:, :],
                                           psS[:, 1024:1536])))(),
                    reads=(bt1, bpsS), writes=(bhg,))
            for dc in range(NCH):
                pbank, bpbank = (psA, bpsA) if dc % 2 == 0 else (psB, bpsB)
                wg2, bwg2 = emit_w_granule(ctx, w2s[e_], dc, B, in_dim=4096,
                                           bufkey="wg2")
                fns = []
                for hc in range(NH):
                    last = (hc == NH - 1) and not (kind == "l1" and e_ == E - 1)
                    fns.append((lambda hc=hc, pbank=pbank, last=last: (
                        lambda e: e.matmul(pbank[:, :], wg2[:, hc * 128:(hc + 1) * 128],
                                           hg[:, hc * 512:(hc + 1) * 512],
                                           start=(hc == 0), stop=last)))())
                reads = [bwg2, bhg]
                if kind == "l1" and e_ == E - 1:
                    fns.append((lambda dc=dc, t0=t0, pbank=pbank: (
                        lambda e: e.matmul(pbank[:, :], B["b2b"][0][0:8, dc * 128:(dc + 1) * 128],
                                           GMT[0:8, t0:t0 + 512], start=False, stop=True)))())
                    reads += [B["b2b"][1], bGMT]
                p.group("pe", fns, reads=tuple(reads), writes=(bpbank,))
                p.op("dve", (lambda dc=dc, pbank=pbank: (
                    lambda e: e.tensor_add(oacc[:, dc * 512:(dc + 1) * 512],
                                           oacc[:, dc * 512:(dc + 1) * 512], pbank[:, :])))(),
                    reads=(bpbank,), writes=(boacc,))
        for dc in range(NCH):
            p.op("dve", (lambda dc=dc, t0=t0: (
                lambda e: e.tensor_add(X[:, dc * S + t0: dc * S + t0 + 512],
                                       X[:, dc * S + t0: dc * S + t0 + 512],
                                       oacc[:, dc * 512:(dc + 1) * 512])))(),
                reads=(boacc,), writes=(bX,))



def build_common(weights_spec, extra_alloc, emit_fn, smw=128, smew=2048,
                 need_brow=True, smedt=F32):
    consts = host_constants()
    nc = bass.Bass("TRN2", target_bir_lowering=False)
    caps = declare_constants(nc, consts)
    x_in = nc.dram_tensor("x_in", [2, S, D], F32, kind="ExternalInput").ap()
    x_out = nc.dram_tensor("x_out", [2, S, D], F32, kind="ExternalOutput").ap()
    Wd = {}
    for nm, shape in weights_spec:
        Wd[nm] = nc.dram_tensor("w_" + nm, shape, F32, kind="ExternalInput").ap()
    ctx = Ctx(nc, caps)
    B = alloc_common(ctx, smw=smw, smew=smew, need_brow=need_brow, smedt=smedt)
    extra_alloc(ctx, B)
    for seq in range(2):
        emit_load_x(ctx, x_in[seq], B["X"][0], B["X"][1], B["stage"][0], B["stage"][1],
                    B["psT32"][0], B["psT32"][1])
        emit_ln(ctx, B)
        emit_fn(ctx, B, Wd)
        emit_store_x(ctx, B["X"][0], B["X"][1], x_out[seq], B["stage"][0], B["stage"][1],
                     B["psT32"][0], B["psT32"][1])
    ctx.p.drain("sp", [B["stage"][1]])
    ctx.p.replay(nc)
    return nc, consts


def alloc_l0_attn_b(ctx, B):
    B["lat"] = ctx.sb("lat", [128, 2 * S], BF16)
    B["qp"] = ctx.sb("qp", [128, S], BF16)
    B["kp"] = ctx.sb("kp", [128, S], BF16)
    B["vtp"] = ctx.sb("vtp", [128, NT * 128], BF16)
    B["vstage"] = ctx.sb("vstage", [128, 512], BF16)
    B["AO"] = ctx.sb("AO", [128, NCH * S], BF16)
    B["PT"] = ctx.sb("PTb", [128, NT * 512], BF16)
    B["Pbuf"] = ctx.sb("Pbuf", [128, S], BF16)


def build_l0_attn_b_program():
    spec = (("Wq", [D, H, HD]), ("Wkv_down", [D, 256]), ("Wk_up", [256, H, HD]),
            ("Wv_up", [256, H, HD]), ("Wout", [H, HD, D]))
    return build_common(spec, alloc_l0_attn_b, emit_l0_attn_b, smw=8,
                        need_brow=False)


def alloc_ffn(ctx, B):
    B["hg"] = ctx.sb("hgbuf", [128, 32 * 512], BF16)
    B["oacc"] = ctx.sb("oacc", [128, 8 * 512], F32)
    B["gates"] = ctx.sb("gates", [128, NT * 8], F32)
    B["GMT"] = ctx.sb("GMT", [128, S], BF16)
    B["grow"] = ctx.sb("grow", [1, S], BF16)
    B["gwT"] = ctx.sb("gwT", [128, NCH * 8], BF16)
    B["wg2"] = ctx.sb("wg2", [128, 4096], BF16)
    B["wstage2"] = ctx.sb("wstage2", [128, 2048], F32)
    B["rwf"] = ctx.sb("rwf", [128, NCH], F32)
    B["b1f"] = ctx.sb("b1f", [128, 8 * 32], F32)
    B["b2b"] = ctx.sb("b2b", [128, D], BF16)


def build_l0_ffn_program():
    spec = (("router_w", [D, 8]), ("up_w", [8, D, 4096]), ("down_w", [8, 4096, D]))
    return build_common(spec, alloc_ffn, lambda ctx, B, W: emit_ffn(ctx, B, W, "l0"),
                        smw=8, smew=8, need_brow=False)


def build_l1_ffn_program():
    spec = (("rms_w", [D]), ("gate_w", [8, D]), ("W1", [8, D, 4096]),
            ("b1", [8, 4096]), ("W2", [8, 4096, D]), ("b2", [8, D]))
    return build_common(spec, alloc_ffn, lambda ctx, B, W: emit_ffn(ctx, B, W, "l1"),
                        smw=8, smew=8, need_brow=False)



def emit_l1_attn_a(ctx, B, W):
    p = ctx.p
    X, bX = B["X"]; XN, bXN = B["XN"]
    AO, bAO = B["AO"]
    qp, bqp = B["qp"]; kp, bkp = B["kp"]
    vf, bvf = B["vf"]; vtp, bvtp = B["vtp"]
    y12, by12 = B["y12"]; y22, by22 = B["y22"]; y1t, by1t = B["y1tok"]
    wsc, bwsc = B["wsc"]; wbc, bwbc = B["wbc"]
    psC, bpsC = B["psC"]
    identb, bidentb = ctx.load_const("ident_bf16")
    onesrow, bonesrow = ctx.load_const("ones_row_f32")
    scale = 1.0 / np.sqrt(HD)
    for i, nm in enumerate(("w0", "w1", "wK")):
        p.dma("pool", wsc[0:1, i * 16:(i + 1) * 16],
              W[nm].rearrange("(o n) -> o n", o=1), writes=(bwsc,), stream="bias")
    p.op("dve", (lambda: (
        lambda e: e.tensor_sub(wsc[0:1, 48:64], wsc[0:1, 16:32], wsc[0:1, 32:48])))(),
        reads=(bwsc,), writes=(bwsc,))
    p.op("dve", (lambda: (
        lambda e: e.tensor_scalar_mul(wsc[0:1, 64:80], wsc[0:1, 32:48], 2.0)))(),
        reads=(bwsc,), writes=(bwsc,))
    for i, s0 in enumerate((0, 48, 64)):
        p.group("pe", [(lambda s0=s0: (
            lambda e: e.matmul(psC[:, 0:16], onesrow[:, :], wsc[0:1, s0:s0 + 16],
                               start=True, stop=True)))()],
            reads=(bwsc, bonesrow), writes=(bpsC,))
        p.op("dve", (lambda i=i: (
            lambda e: e.tensor_copy(wbc[:, i * 16:(i + 1) * 16], psC[:, 0:16])))(),
            reads=(bpsC,), writes=(bwbc,))

    for i, bn in enumerate(("bq", "bk", "bv", "bo")):
        emit_bias_row(ctx, W[bn], i, D, B)
    brow, bbrow = B["brow"]

    for c in range(NCH):
        emit_pair_proj(ctx, B, W["Wq"], (qp, bqp), c, (XN, bXN), transposed=False,
                       bias_slice=brow[0:1, 0 * D + c * 128: 0 * D + (c + 1) * 128])
        emit_pair_proj(ctx, B, W["Wk"], (kp, bkp), c, (XN, bXN), transposed=False,
                       bias_slice=brow[0:1, 1 * D + c * 128: 1 * D + (c + 1) * 128])
        emit_pair_proj(ctx, B, W["Wv"], (vf, bvf), c, (XN, bXN), transposed=False,
                       bias_slice=brow[0:1, 2 * D + c * 128: 2 * D + (c + 1) * 128],
                       to_tok=(vtp, bvtp))
        for hh in range(2):
            rb = hh * 64
            h = 2 * c + hh
            qtile_ap = (lambda rb=rb: (lambda tt: qp[rb:rb + 64, tt * 128:(tt + 1) * 128]))()
            vtile1 = (lambda rb=rb: (lambda kt: vtp[:, kt * 128 + rb: kt * 128 + rb + 64]))()
            def wo1(qg, ps, bps, rb=rb):
                p.op("dve", (lambda qg=qg, ps=ps, rb=rb: (
                    lambda e: e.tensor_copy(y12[rb:rb + 64, qg * 512:(qg + 1) * 512], ps[0:64, :])))(),
                    reads=(bps,), writes=(by12,))
            emit_attn_core(ctx, B, qtile_ap, kp[rb:rb + 64, :], bkp, vtile1, bvtp,
                           wo1, scale, causal=False, use_softmax=True, bq_src=bqp)
            psTb, bpsTb = B["psTbf"]
            for tt in range(NT):
                p.op("pe", (lambda tt=tt, rb=rb: (
                    lambda e: e.transpose(psTb[:, 0:64], y12[rb:rb + 64, tt * 128:(tt + 1) * 128],
                                          identb[rb:rb + 64, rb:rb + 64])))(),
                    reads=(by12, bidentb), writes=(bpsTb,))
                p.op("dve", (lambda tt=tt: (
                    lambda e: e.tensor_copy(y1t[:, tt * 64:(tt + 1) * 64], psTb[:, 0:64])))(),
                    reads=(bpsTb,), writes=(by1t,))
            vtile2 = (lambda: (lambda kt: y1t[:, kt * 64:(kt + 1) * 64]))()
            def wo2(qg, ps, bps, rb=rb):
                p.op("dve", (lambda qg=qg, ps=ps, rb=rb: (
                    lambda e: e.tensor_copy(y22[rb:rb + 64, qg * 512:(qg + 1) * 512], ps[0:64, :])))(),
                    reads=(bps,), writes=(by22,))
            emit_attn_core(ctx, B, qtile_ap, kp[rb:rb + 64, :], bkp, vtile2, by1t,
                           wo2, scale, causal=False, use_softmax=True, bq_src=bqp)
            for tg in range(NTG):
                t0 = tg * 512
                p.op("dve", (lambda rb=rb, c=c, t0=t0, h=h: (
                    lambda e: e.tensor_scalar_mul(AO[rb:rb + 64, c * S + t0: c * S + t0 + 512],
                                                  vf[rb:rb + 64, t0:t0 + 512],
                                                  wbc[rb:rb + 64, h:h + 1])))(),
                    reads=(bvf, bwbc), writes=(bAO,))
                p.op("dve", (lambda rb=rb, c=c, t0=t0, h=h: (
                    lambda e: e.scalar_tensor_tensor(AO[rb:rb + 64, c * S + t0: c * S + t0 + 512],
                                                     y12[rb:rb + 64, t0:t0 + 512],
                                                     wbc[rb:rb + 64, 16 + h:17 + h],
                                                     AO[rb:rb + 64, c * S + t0: c * S + t0 + 512],
                                                     ALU.mult, ALU.add)))(),
                    reads=(by12, bwbc), writes=(bAO,))
                p.op("dve", (lambda rb=rb, c=c, t0=t0, h=h: (
                    lambda e: e.scalar_tensor_tensor(AO[rb:rb + 64, c * S + t0: c * S + t0 + 512],
                                                     y22[rb:rb + 64, t0:t0 + 512],
                                                     wbc[rb:rb + 64, 32 + h:33 + h],
                                                     AO[rb:rb + 64, c * S + t0: c * S + t0 + 512],
                                                     ALU.mult, ALU.add)))(),
                    reads=(by22, bwbc), writes=(bAO,))
    emit_out_proj_residual(ctx, B, W["Wo"], 3, "AO", transposed=False)


def alloc_l1_attn_a(ctx, B):
    B["qp"] = ctx.sb("qp", [128, S], BF16)
    B["kp"] = ctx.sb("kp", [128, S], BF16)
    B["vf"] = ctx.sb("vf", [128, S], BF16)
    B["vtp"] = ctx.sb("vtp", [128, NT * 128], BF16)
    B["AO"] = ctx.sb("AO", [128, NCH * S], BF16)
    B["PT"] = ctx.sb("PTb", [128, NT * 512], BF16)
    B["Pbuf"] = ctx.sb("Pbuf", [128, S], BF16)
    B["y12"] = ctx.sb("y12", [128, S], BF16)
    B["y22"] = ctx.sb("y22", [128, S], BF16)
    B["y1tok"] = ctx.sb("y1tok", [128, NT * 64], BF16)
    B["wsc"] = ctx.sb("wsc", [1, 128], F32)
    B["wbc"] = ctx.sb("wbc", [128, 48], F32)


def build_l1_attn_a_program():
    spec = (("Wq", [D, D]), ("bq", [D]), ("Wk", [D, D]), ("bk", [D]),
            ("Wv", [D, D]), ("bv", [D]), ("Wo", [D, D]), ("bo", [D]),
            ("w0", [H]), ("w1", [H]), ("wK", [H]))
    return build_common(spec, alloc_l1_attn_a, emit_l1_attn_a, smw=8, smedt=BF16,
                        need_brow="slim")



def emit_l1_attn_b(ctx, B, W):
    p = ctx.p
    X, bX = B["X"]; XN, bXN = B["XN"]
    AO, bAO = B["AO"]
    qh, bqh = B["qh"]; kh, bkh = B["kh"]
    kap, bkap = B["kap"]; qap, bqap = B["qap"]
    xth, bxth = B["xtok_h"]; ef, bef = B["e_feat"]; eth, beth = B["e_tok"]
    t1, bt1 = B["t1"]
    psTb, bpsTb = B["psTbf"]
    identb, bidentb = ctx.load_const("ident_bf16")
    brow, bbrow = B["brow"]
    rowstage, browstage = B["rowstage"]
    scale = 1.0 / np.sqrt(HD)
    p.dma("pool", rowstage[0:1, 0:D], W["b_attn"].rearrange("(a n) -> a n", a=2)[0:1, :],
          writes=(browstage,), stream="bias")
    p.op("dve", (lambda: (lambda e: e.tensor_copy(brow[0:1, 0:D], rowstage[0:1, 0:D])))(),
         reads=(browstage,), writes=(bbrow,))
    p.dma("pool", rowstage[0:1, 0:D], W["b_attn"].rearrange("(a n) -> a n", a=2)[1:2, :],
          writes=(browstage,), stream="bias")
    p.op("dve", (lambda: (lambda e: e.tensor_copy(brow[0:1, D:2 * D], rowstage[0:1, 0:D])))(),
         reads=(browstage,), writes=(bbrow,))
    p.dma("pool", rowstage[0:1, 0:D], W["b_k2"].rearrange("(o n) -> o n", o=1),
          writes=(browstage,), stream="bias")
    p.op("dve", (lambda: (lambda e: e.tensor_copy(brow[0:1, 2 * D:3 * D], rowstage[0:1, 0:D])))(),
         reads=(browstage,), writes=(bbrow,))
    p.dma("pool", rowstage[0:1, 0:D], W["b_proj"].rearrange("(o n) -> o n", o=1),
          writes=(browstage,), stream="bias")
    p.op("dve", (lambda: (
        lambda e: e.tensor_scalar_mul(brow[0:1, 3 * D:4 * D], rowstage[0:1, 0:D], 2.0)))(),
        reads=(browstage,), writes=(bbrow,))

    for c in range(NCH):
        emit_pair_proj(ctx, B, W["w_attn"], (qh, bqh), c, (XN, bXN), transposed=True,
                       bias_slice=brow[0:1, c * 128:(c + 1) * 128])
        emit_pair_proj(ctx, B, W["w_attn"], (kh, bkh), 8 + c, (XN, bXN), transposed=True,
                       bias_slice=brow[0:1, D + c * 128: D + (c + 1) * 128])
        emit_pair_proj(ctx, B, W["w_k2"], (kap, bkap), c, (XN, bXN), transposed=True,
                       bias_slice=brow[0:1, 2 * D + c * 128: 2 * D + (c + 1) * 128],
                       evac=(AF.Sigmoid, scale * 0.02))
        for tg in range(NTG):
            t0 = tg * 512
            p.op("dve", (lambda t0=t0: (
                lambda e: e.tensor_scalar_mul(t1[:, :], qh[:, t0:t0 + 512], float(scale * 0.02))))(),
                reads=(bqh,), writes=(bt1,))
            p.op("dve", (lambda t0=t0: (
                lambda e: e.tensor_scalar_mul(B["sq"][0][:, :], qh[:, t0:t0 + 512], float(scale))))(),
                reads=(bqh,), writes=(B["sq"][1],))
            p.op("dve", (lambda t0=t0: (
                lambda e: e.tensor_tensor(qap[:, t0:t0 + 512], B["sq"][0][:, :], t1[:, :], ALU.min)))(),
                reads=(B["sq"][1], bt1), writes=(bqap,))
        for hh in range(2):
            rb = hh * 64
            for tt in range(NT):
                p.op("pe", (lambda tt=tt, rb=rb, c=c: (
                    lambda e: e.transpose(psTb[:, 0:64],
                                          XN[rb:rb + 64, c * S + tt * 128: c * S + (tt + 1) * 128],
                                          identb[rb:rb + 64, rb:rb + 64])))(),
                    reads=(bXN, bidentb), writes=(bpsTb,))
                p.op("dve", (lambda tt=tt: (
                    lambda e: e.tensor_copy(xth[:, tt * 64:(tt + 1) * 64], psTb[:, 0:64])))(),
                    reads=(bpsTb,), writes=(bxth,))
            qtile_ap = (lambda rb=rb: (lambda tt: qh[rb:rb + 64, tt * 128:(tt + 1) * 128]))()
            vtile = (lambda: (lambda kt: xth[:, kt * 64:(kt + 1) * 64]))()
            def wo_ar(qg, ps, bps, c=c, rb=rb):
                p.op("dve", (lambda qg=qg, c=c, rb=rb, ps=ps: (
                    lambda e: e.tensor_copy(AO[rb:rb + 64, c * S + qg * 512: c * S + qg * 512 + 512],
                                            ps[0:64, :])))(),
                    reads=(bps,), writes=(bAO,))
            emit_attn_core(ctx, B, qtile_ap, kh[rb:rb + 64, :], bkh, vtile, bxth,
                           wo_ar, scale, causal=True, use_softmax=True, bq_src=bqh)
            p.op("dve", (lambda c=c, rb=rb: (
                lambda e: e.tensor_sub(ef[0:64, 0:S - 1],
                                       XN[rb:rb + 64, c * S + 1: c * S + S],
                                       AO[rb:rb + 64, c * S: c * S + S - 1])))(),
                reads=(bXN, bAO), writes=(bef,))
            p.op("dve", (lambda: (lambda e: e.memset(ef[0:64, S - 1:S], 0.0)))(),
                 writes=(bef,))
            for tt in range(NT):
                p.op("pe", (lambda tt=tt: (
                    lambda e: e.transpose(psTb[:, 0:64], ef[0:64, tt * 128:(tt + 1) * 128],
                                          identb[0:64, 0:64])))(),
                    reads=(bef, bidentb), writes=(bpsTb,))
                p.op("dve", (lambda tt=tt: (
                    lambda e: e.tensor_copy(eth[:, tt * 64:(tt + 1) * 64], psTb[:, 0:64])))(),
                    reads=(bpsTb,), writes=(beth,))
            qtile2 = (lambda rb=rb: (lambda tt: qap[rb:rb + 64, tt * 128:(tt + 1) * 128]))()
            vtile2 = (lambda: (lambda kt: eth[:, kt * 64:(kt + 1) * 64]))()
            def wo_ma(qg, ps, bps, c=c, rb=rb):
                p.op("dve", (lambda qg=qg, c=c, rb=rb, ps=ps: (
                    lambda e: e.tensor_add(AO[rb:rb + 64, c * S + qg * 512: c * S + qg * 512 + 512],
                                           AO[rb:rb + 64, c * S + qg * 512: c * S + qg * 512 + 512],
                                           ps[0:64, :])))(),
                    reads=(bps,), writes=(bAO,))
            emit_attn_core(ctx, B, qtile2, kap[rb:rb + 64, :], bkap, vtile2, beth,
                           wo_ma, 1.0, causal=True, use_softmax=False, bq_src=bqap)
    emit_out_proj_residual(ctx, B, W["w_proj"], 3, "AO", transposed=True)


def alloc_l1_attn_b(ctx, B):
    B["qh"] = ctx.sb("qh", [128, S], BF16)
    B["kh"] = ctx.sb("kh", [128, S], BF16)
    B["kap"] = ctx.sb("kap", [128, S], BF16)
    B["qap"] = ctx.sb("qap", [128, S], BF16)
    B["xtok_h"] = ctx.sb("xtok_h", [128, NT * 64], BF16)
    B["e_feat"] = ctx.sb("e_feat", [64, S], BF16)
    B["e_tok"] = B["xtok_h"]
    B["vstage"] = ctx.sb("vstage", [128, 512], BF16)
    B["AO"] = ctx.sb("AO", [128, NCH * S], BF16)
    B["PT"] = ctx.sb("PTb", [128, NT * 512], BF16)
    B["Pbuf"] = ctx.sb("Pbuf", [128, S], BF16)


def build_l1_attn_b_program():
    spec = (("w_attn", [2 * D, D]), ("b_attn", [2 * D]), ("w_k2", [D, D]),
            ("b_k2", [D]), ("w_proj", [D, D]), ("b_proj", [D]))
    return build_common(spec, alloc_l1_attn_b, emit_l1_attn_b, smw=8, smedt=BF16,
                        need_brow="slim")



_SUBLAYERS = (
    ("l0_attn_a", "build_l0_attn_a_program",
     {"Wq": "L0_attn_a_Wq", "bq": "L0_attn_a_bq", "Wk": "L0_attn_a_Wk",
      "bk": "L0_attn_a_bk", "Wv": "L0_attn_a_Wv", "bv": "L0_attn_a_bv",
      "Wo": "L0_attn_a_Wo", "bo": "L0_attn_a_bo"}),
    ("l0_ffn", "build_l0_ffn_program",
     {"router_w": "L0_ffn_router_w", "up_w": "L0_ffn_up_w", "down_w": "L0_ffn_down_w"}),
    ("l0_attn_b", "build_l0_attn_b_program",
     {"Wq": "L0_attn_b_Wq", "Wkv_down": "L0_attn_b_Wkv_down",
      "Wk_up": "L0_attn_b_Wk_up", "Wv_up": "L0_attn_b_Wv_up",
      "Wout": "L0_attn_b_Wout"}),
    ("l0_ffn", None, None),
    ("l1_attn_a", "build_l1_attn_a_program",
     {"Wq": "L1_attn_a_Wq", "bq": "L1_attn_a_bq", "Wk": "L1_attn_a_Wk",
      "bk": "L1_attn_a_bk", "Wv": "L1_attn_a_Wv", "bv": "L1_attn_a_bv",
      "Wo": "L1_attn_a_Wo", "bo": "L1_attn_a_bo", "w0": "L1_attn_a_w0",
      "w1": "L1_attn_a_w1", "wK": "L1_attn_a_wK"}),
    ("l1_ffn", "build_l1_ffn_program",
     {"rms_w": "L1_ffn_rms_w", "gate_w": "L1_ffn_gate_w", "W1": "L1_ffn_W1",
      "b1": "L1_ffn_b1", "W2": "L1_ffn_W2", "b2": "L1_ffn_b2"}),
    ("l1_attn_b", "build_l1_attn_b_program",
     {"w_attn": "L1_attn_b_w_attn", "b_attn": "L1_attn_b_b_attn",
      "w_k2": "L1_attn_b_w_k2", "b_k2": "L1_attn_b_b_k2",
      "w_proj": "L1_attn_b_w_proj", "b_proj": "L1_attn_b_b_proj"}),
    ("l1_ffn", None, None),
)


def kernel(**inputs):
    x = np.ascontiguousarray(inputs["L0_attn_a_x"], dtype=np.float32)
    ncores = 8
    xs = [np.ascontiguousarray(x[2 * c:2 * c + 2]) for c in range(ncores)]
    built = {}
    maps = {}
    for name, bfn, wmap in _SUBLAYERS:
        if bfn is None:
            continue
        nc_prog, consts = globals()[bfn]()
        cmap = {"c_" + k: np.asarray(v) for k, v in consts.items()}
        built[name] = (nc_prog, cmap)
        maps[name] = wmap
    for name, bfn, wmap in _SUBLAYERS:
        nc_prog, cmap = built[name]
        wmap = maps[name]
        wvals = {"w_" + k: np.ascontiguousarray(inputs[v], dtype=np.float32)
                 for k, v in wmap.items()}
        in_maps = []
        for c in range(ncores):
            im = {"x_in": xs[c]}
            im.update(cmap)
            im.update(wvals)
            in_maps.append(im)
        res = run_bass_kernel_spmd(nc_prog, in_maps, core_ids=list(range(ncores)))
        xs = [np.asarray(res.results[c]["x_out"], dtype=np.float32)
              for c in range(ncores)]
    return np.concatenate(xs, axis=0)


# version 02 (final)
# speedup vs baseline: 1.0760x; 1.0760x over previous
import numpy as np
import ml_dtypes
from collections import defaultdict
import concourse.bass as bass
import concourse.mybir as mybir
from concourse.bass_utils import run_bass_kernel_spmd

F32 = mybir.dt.float32
BF16 = mybir.dt.bfloat16
AF = mybir.ActivationFunctionType
ALU = mybir.AluOpType
AX = mybir.AxisListType

S = 2048
D = 1024
NCH = D // 128
NT = S // 128
NTG = S // 512
H = 16
HD = 64
NEG = -1.0e30

ENGINES = ("pe", "act", "dve", "pool", "sp")
DMAQ = ("sp", "pool", "act", "dve")


class Buf:
    __slots__ = ("name", "w", "r")

    def __init__(self, name):
        self.name = name
        self.w = []
        self.r = []


class Prog:
    def __init__(self, strict=False):
        self.plan = {e: [] for e in ENGINES}
        self.cnt = defaultdict(int)
        self.mark = {e: defaultdict(int) for e in ENGINES}
        self.strict = strict
        self.allbuf = Buf("__all__")
        self.dmasems = set()

    def _waits(self, eng, reads, writes):
        need = {}
        for b in reads:
            for (s, v) in b.w:
                if need.get(s, 0) < v:
                    need[s] = v
        for b in writes:
            for (s, v) in b.w + b.r:
                if need.get(s, 0) < v:
                    need[s] = v
        out = []
        for s, v in sorted(need.items()):
            if s.startswith("D"):
                v = self.cnt[s]
            if self.mark[eng][s] < v:
                out.append((s, v))
                self.mark[eng][s] = v
        return out

    def _record(self, tok, reads, writes):
        for b in writes:
            b.w = [tok]
            b.r = []
        for b in reads:
            b.r.append(tok)

    def op(self, eng, fn, reads=(), writes=()):
        if self.strict:
            writes = tuple(writes) + (self.allbuf,)
        waits = self._waits(eng, reads, writes)
        self.cnt[eng] += 1
        tok = (eng, self.cnt[eng])
        self.plan[eng].append((waits, fn, eng, 1))
        self._record(tok, reads, writes)
        return tok

    def group(self, eng, fns, reads=(), writes=()):
        if self.strict:
            writes = tuple(writes) + (self.allbuf,)
        waits = self._waits(eng, reads, writes)
        self.cnt[eng] += 1
        tok = (eng, self.cnt[eng])
        for i, fn in enumerate(fns):
            last = i == len(fns) - 1
            self.plan[eng].append((waits if i == 0 else [], fn,
                                   eng if last else None, 1))
        self._record(tok, reads, writes)
        return tok

    def dma(self, q, out_ap, in_ap, reads=(), writes=(), stream="", **kw):
        if self.strict:
            writes = tuple(writes) + (self.allbuf,)
        waits = self._waits(q, reads, writes)
        sem = "D" + q + (("_" + stream) if stream else "")
        self.dmasems.add(sem)
        self.cnt[sem] += 16
        tok = (sem, self.cnt[sem])
        self.plan[q].append((waits, ("dma", out_ap, in_ap, kw), sem, 16))
        self._record(tok, reads, writes)
        return tok

    def drain(self, eng, bufs):
        waits = self._waits(eng, (), bufs)
        self.plan[eng].append((waits, None, None, 0))

    def replay(self, nc):
        sems = {}
        for e in ENGINES:
            sems[e] = nc.alloc_semaphore("s_" + e)
        for q in self.dmasems:
            sems[q] = nc.alloc_semaphore("s" + q)
        plan = self.plan

        cnt = self.cnt

        def body(ename):
            def run(eng):
                for waits, fn, incsem, incamt in plan[ename]:
                    for (s, v) in waits:
                        eng.wait_ge(sems[s], v)
                    if fn is None:
                        eng.nop()
                        continue
                    if isinstance(fn, tuple):
                        ins = eng.dma_start(fn[1], fn[2], **fn[3])
                    else:
                        ins = fn(eng)
                    if incsem is not None:
                        ins.then_inc(sems[incsem], incamt)
                if ename == "sp":
                    for q in sorted(self.dmasems):
                        if cnt[q] > 0:
                            eng.wait_ge(sems[q], cnt[q])
            return run

        with nc.Block() as blk:
            blk.tensor(body("pe"))
            blk.scalar(body("act"))
            blk.vector(body("dve"))
            blk.gpsimd(body("pool"))
            blk.sync(body("sp"))



def host_constants():
    c = {}
    c["ident_f32"] = np.eye(128, dtype=np.float32)
    c["ident_bf16"] = np.eye(128, dtype=ml_dtypes.bfloat16)
    c["ones_col"] = np.ones((128, 1), dtype=np.float32)
    c["ones_row_f32"] = np.ones((1, 128), dtype=np.float32)
    c["ones_row_bf16"] = np.ones((1, 512), dtype=ml_dtypes.bfloat16)
    bs = 128
    q_off = np.arange(bs)
    k_off = np.arange(3 * bs)
    rel = k_off[None, :] - bs - q_off[:, None]
    c["mask_mid"] = np.where(np.abs(rel) <= bs, 0.0, NEG).astype(np.float32)
    k_off2 = np.arange(2 * bs)
    rel_f = k_off2[None, :] - q_off[:, None]
    c["mask_first"] = np.where(np.abs(rel_f) <= bs, 0.0, NEG).astype(np.float32)
    rel_l = k_off2[None, :] - bs - q_off[:, None]
    c["mask_last"] = np.where(np.abs(rel_l) <= bs, 0.0, NEG).astype(np.float32)
    tri = np.tril(np.ones((128, 128), dtype=bool))
    c["mask_tril"] = np.where(tri, 0.0, NEG).astype(ml_dtypes.bfloat16)
    stri = np.tril(np.ones((128, 128), dtype=bool), k=-1)
    c["mask_stril_f32"] = stri.astype(ml_dtypes.bfloat16)
    return c


def declare_constants(nc, c):
    aps = {}
    for k, v in c.items():
        dt = BF16 if v.dtype == ml_dtypes.bfloat16 else F32
        aps[k] = nc.dram_tensor("c_" + k, list(v.shape), dt, kind="ExternalInput").ap()
    return aps


STRICT = [False]
DBG = {}


class Ctx:
    def __init__(self, nc, const_aps):
        self.nc = nc
        self.p = Prog(strict=STRICT[0])
        self.caps = const_aps
        self.bufs = {}

    def sb(self, name, shape, dtype):
        t = self.nc.alloc_sbuf_tensor(name, shape, dtype)
        b = Buf(name)
        self.bufs[name] = (t, b)
        return t, b

    def ps(self, name, shape, dtype):
        t = self.nc.alloc_psum_tensor(name, shape, dtype)
        b = Buf(name)
        self.bufs[name] = (t, b)
        return t, b

    def load_const(self, key):
        name = "k_" + key
        if name in self.bufs:
            return self.bufs[name]
        ap = self.caps[key]
        t = self.nc.alloc_sbuf_tensor(name, list(ap.shape), ap.dtype)
        b = Buf(name)
        self.bufs[name] = (t, b)
        self.p.dma("sp", t[:, :], ap[:, :], writes=(b,), stream="c_" + key)
        return t, b

    def preload(self):
        for key in self.caps:
            self.load_const(key)

    def constcell(self, value):
        name = f"kc_{value!r}"
        if name in self.bufs:
            return self.bufs[name]
        t = self.nc.alloc_sbuf_tensor(name.replace('-', 'm').replace('.', '_'),
                                      [128, 1], F32)
        b = Buf(name)
        self.bufs[name] = (t, b)
        self.p.op("dve", (lambda t=t, value=value: (
            lambda e: e.memset(t[:, :], float(value))))(), writes=(b,))
        return t, b



def emit_load_x(ctx, x_dram_seq, X, bX, stage, bstage, psT32, bpsT32):
    p = ctx.p
    ident, bident = ctx.load_const("ident_f32")
    for tt in range(NT):
        p.dma("sp", stage[:, :], x_dram_seq[tt * 128:(tt + 1) * 128, :],
              writes=(bstage,), stream="io")
        for c in range(NCH):
            p.op("pe", (lambda c=c: (
                lambda e: e.transpose(psT32[:, 0:128], stage[:, c * 128:(c + 1) * 128], ident[:, :])))(),
                reads=(bstage, bident), writes=(bpsT32,))
            p.op("dve", (lambda c=c, tt=tt: (
                lambda e: e.tensor_copy(X[:, c * S + tt * 128: c * S + (tt + 1) * 128], psT32[:, 0:128])))(),
                reads=(bpsT32,), writes=(bX,))


def emit_store_x(ctx, X, bX, y_dram_seq, stage, bstage, psT32, bpsT32):
    p = ctx.p
    ident, bident = ctx.load_const("ident_f32")
    for tt in range(NT):
        for c in range(NCH):
            p.op("pe", (lambda c=c, tt=tt: (
                lambda e: e.transpose(psT32[:, 0:128], X[:, c * S + tt * 128: c * S + (tt + 1) * 128], ident[:, :])))(),
                reads=(bX, bident), writes=(bpsT32,))
            p.op("dve", (lambda c=c: (
                lambda e: e.tensor_copy(stage[:, c * 128:(c + 1) * 128], psT32[:, 0:128])))(),
                reads=(bpsT32,), writes=(bstage,))
        p.dma("sp", y_dram_seq[tt * 128:(tt + 1) * 128, :], stage[:, :],
              reads=(bstage,), writes=(), stream="io")


def emit_ln(ctx, B, eps=1e-5):
    p = ctx.p
    X, bX = B["X"]; XN, bXN = B["XN"]
    onescol, bonescol = ctx.load_const("ones_col")
    onesrow, bonesrow = ctx.load_const("ones_row_f32")
    ps1, bps1 = B["psB"]
    ps2, bps2 = B["psC"]
    pmb, bpmb = B["psA"]
    prb, bprb = B["psSr3"]
    sq, bsq = B["sq"]
    rows, brows = B["rows"]
    t1, bt1 = B["t1"]
    for tg in range(NTG):
        t0 = tg * 512
        fns1 = [(lambda c=c, t0=t0: (
            lambda e: e.matmul(ps1[0:1, :], onescol[:, :], X[:, c * S + t0: c * S + t0 + 512],
                               start=(c == 0), stop=(c == NCH - 1))))() for c in range(NCH)]
        p.group("pe", fns1, reads=(bX, bonescol), writes=(bps1,))
        for c in range(NCH):
            p.op("act", (lambda c=c, t0=t0: (
                lambda e: e.activation(sq[:, :], X[:, c * S + t0: c * S + t0 + 512], AF.Square)))(),
                reads=(bX,), writes=(bsq,))
            p.group("pe", [(lambda c=c: (
                lambda e: e.matmul(ps2[0:1, :], onescol[:, :], sq[:, :],
                                   start=(c == 0), stop=(c == NCH - 1))))()],
                reads=(bsq, bonescol), writes=(bps2,))
        m = rows[0:1, 0:512]; v = rows[0:1, 512:1024]
        sr = rows[0:1, 1024:1536]; r = rows[0:1, 1536:2048]
        p.op("dve", (lambda m=m: (lambda e: e.tensor_scalar_mul(m, ps1[0:1, :], 1.0 / D)))(),
             reads=(bps1,), writes=(brows,))
        p.op("dve", (lambda v=v: (lambda e: e.tensor_scalar_mul(v, ps2[0:1, :], 1.0 / D)))(),
             reads=(bps2,), writes=(brows,))
        p.op("dve", (lambda sr=sr, m=m: (lambda e: e.tensor_mul(sr, m, m)))(),
             reads=(brows,), writes=(brows,))
        p.op("dve", (lambda v=v, sr=sr: (lambda e: e.tensor_sub(v, v, sr)))(),
             reads=(brows,), writes=(brows,))
        epsc, bepsc = ctx.constcell(eps)
        p.op("act", (lambda sr=sr, v=v: (lambda e: e.activation(sr, v, AF.Sqrt, bias=epsc[0:1, 0:1])))(),
             reads=(brows, bepsc), writes=(brows,))
        p.op("dve", (lambda r=r, sr=sr: (lambda e: e.reciprocal(r, sr)))(),
             reads=(brows,), writes=(brows,))
        p.group("pe", [(lambda m=m: (
            lambda e: e.matmul(pmb[:, 0:512], onesrow[:, :], m, start=True, stop=True)))()],
            reads=(brows, bonesrow), writes=(bpmb,))
        p.group("pe", [(lambda r=r: (
            lambda e: e.matmul(prb[:, 1536:2048], onesrow[:, :], r, start=True, stop=True)))()],
            reads=(brows, bonesrow), writes=(bprb,))
        for c in range(NCH):
            p.op("dve", (lambda c=c, t0=t0: (
                lambda e: e.tensor_sub(t1[:, :], X[:, c * S + t0: c * S + t0 + 512], pmb[:, 0:512])))(),
                reads=(bX, bpmb), writes=(bt1,))
            p.op("dve", (lambda c=c, t0=t0: (
                lambda e: e.tensor_mul(XN[:, c * S + t0: c * S + t0 + 512], t1[:, :], prb[:, 1536:2048])))(),
                reads=(bt1, bprb), writes=(bXN,))


def emit_wt_granule_T(ctx, w_dram, oc, B, in_dim=D, q="pool"):
    p = ctx.p
    wg, bwg = B["wg"]; stage, bstage = B["wstage"]
    psT32, bpsT32 = B["psT32"]
    ident, bident = ctx.load_const("ident_f32")
    nic = in_dim // 128
    p.dma(q, stage[:, 0:in_dim], w_dram[oc * 128:(oc + 1) * 128, 0:in_dim],
          writes=(bstage,), stream="wgt")
    for ic in range(nic):
        p.op("pe", (lambda ic=ic: (
            lambda e: e.transpose(psT32[:, 0:128], stage[:, ic * 128:(ic + 1) * 128], ident[:, :])))(),
            reads=(bstage, bident), writes=(bpsT32,))
        p.op("dve", (lambda ic=ic: (
            lambda e: e.tensor_copy(wg[:, ic * 128:(ic + 1) * 128], psT32[:, 0:128])))(),
            reads=(bpsT32,), writes=(bwg,))
    return wg, bwg


def emit_w_granule(ctx, w_dram, oc, B, in_dim=D, out_w=128, q="pool",
                   bufkey="wg"):
    p = ctx.p
    wg, bwg = B[bufkey]
    skey = {"wg": "wstage", "wgB": "wstageB", "wg2": "wstage2",
            "wg2B": "wstage2B"}[bufkey]
    stage, bstage = B[skey]
    nic = in_dim // 128
    stage_cols = stage.shape[1]
    per = stage_cols // out_w
    srcfull = w_dram.rearrange("(a p) o -> p a o", p=128)
    strm = "s_" + bufkey
    for a0 in range(0, nic, per):
        an = min(per, nic - a0)
        srcap = srcfull[:, a0:a0 + an, oc * out_w:(oc + 1) * out_w]
        p.dma(q, stage.rearrange("p (a o) -> p a o", o=out_w)[:, 0:an, :], srcap,
              writes=(bstage,), stream=strm)
        p.op("act", (lambda a0=a0, an=an, wg=wg, stage=stage, n=an * out_w: (
            lambda e: e.activation(wg[:, a0 * out_w: a0 * out_w + n], stage[:, 0:n], AF.Copy)))(),
            reads=(bstage,), writes=(bwg,))
    return wg, bwg


def emit_bias_row(ctx, v_dram, idx, n, B, q="pool"):
    p = ctx.p
    brow, bbrow = B["brow"]; stage, bstage = B["rowstage"]
    p.dma(q, stage[0:1, 0:n], v_dram.rearrange("(o n) -> o n", o=1), writes=(bstage,), stream="bias")
    p.op("dve", (lambda idx=idx, n=n: (
        lambda e: e.tensor_copy(brow[0:1, idx * D: idx * D + n], stage[0:1, 0:n])))(),
        reads=(bstage,), writes=(bbrow,))


def softmax_tile(ctx, ps_in, bps_in, Wd, mask_ap, bmask, scale, out_bf, bout, B):
    p = ctx.p
    sS, bsS = B["smS"]
    sE, bsE = B["smE"]
    rows, brows = B["smrows"]
    if mask_ap is not None:
        p.op("dve", (lambda: (
            lambda e: e.tensor_add(sS[:, 0:Wd], ps_in[:, 0:Wd], mask_ap)))(),
            reads=(bps_in, bmask), writes=(bsS,))
        src, bsrc = sS, bsS
    else:
        src, bsrc = ps_in, bps_in
    p.op("dve", (lambda: (
        lambda e: e.reduce_max(rows[:, 0:1], src[:, 0:Wd], AX.X, negate=True)))(),
        reads=(bsrc,), writes=(brows,))
    p.op("dve", (lambda: (
        lambda e: e.tensor_scalar_mul(rows[:, 1:2], rows[:, 0:1], float(scale))))(),
        reads=(brows,), writes=(brows,))
    p.op("act", (lambda: (
        lambda e: e.activation(sE[:, 0:Wd], src[:, 0:Wd], AF.Exp,
                               bias=rows[:, 1:2], scale=float(scale))))(),
        reads=(bsrc, brows), writes=(bsE,))
    p.op("dve", (lambda: (
        lambda e: e.reduce_sum(rows[:, 2:3], sE[:, 0:Wd], AX.X)))(),
        reads=(bsE,), writes=(brows,))
    p.op("dve", (lambda: (
        lambda e: e.reciprocal(rows[:, 3:4], rows[:, 2:3])))(),
        reads=(brows,), writes=(brows,))
    p.op("dve", (lambda: (
        lambda e: e.tensor_scalar_mul(out_bf[:, 0:Wd], sE[:, 0:Wd], rows[:, 3:4])))(),
        reads=(bsE, brows), writes=(bout,))


def emit_out_proj_residual(ctx, B, w_dram, bias_idx, src_key="AO", transposed=True):
    p = ctx.p
    X, bX = B["X"]
    AO, bAO = B[src_key]
    psA, bpsA = B["psA"]
    brow, bbrow = B.get("brow", (None, None))
    onesrow_b, bonesrow_b = ctx.load_const("ones_row_bf16")
    for oc in range(NCH):
        if transposed:
            wg, bwg = emit_wt_granule_T(ctx, w_dram, oc, B)
        else:
            wg, bwg = emit_w_granule(ctx, w_dram, oc, B)
        for tg in range(NTG):
            t0 = tg * 512
            fns = []
            for ic in range(NCH):
                fns.append((lambda ic=ic, t0=t0, last=(bias_idx is None and ic == NCH - 1): (
                    lambda e: e.matmul(psA[:, :], wg[:, ic * 128:(ic + 1) * 128],
                                       AO[:, ic * S + t0: ic * S + t0 + 512],
                                       start=(ic == 0), stop=last)))())
            if bias_idx is not None:
                fns.append((lambda oc=oc, bias_idx=bias_idx: (
                    lambda e: e.matmul(psA[:, :], brow[0:1, bias_idx * D + oc * 128: bias_idx * D + (oc + 1) * 128],
                                       onesrow_b[0:1, :], start=False, stop=True)))())
            rds = (bwg, bAO, bonesrow_b) + ((bbrow,) if bbrow is not None else ())
            p.group("pe", fns, reads=rds, writes=(bpsA,))
            p.op("dve", (lambda oc=oc, t0=t0: (
                lambda e: e.tensor_add(X[:, oc * S + t0: oc * S + t0 + 512],
                                       X[:, oc * S + t0: oc * S + t0 + 512], psA[:, :])))(),
                reads=(bpsA,), writes=(bX,))



def emit_l0_attn_a(ctx, B, W):
    p = ctx.p
    X, bX = B["X"]; XN, bXN = B["XN"]
    qp, bqp = B["qp"]; kp, bkp = B["kp"]; vtp, bvtp = B["vtp"]
    vstage, bvstage = B["vstage"]
    AO, bAO = B["AO"]
    PT, bPT = B["PT"]; Pb, bPb = B["Pbuf"]
    psS, bpsS = B["psS"]; psA, bpsA = B["psA"]
    psB, bpsB = B["psB"]; psC, bpsC = B["psC"]
    psTb, bpsTb = B["psTbf"]
    brow, bbrow = B["brow"]
    identb, bidentb = ctx.load_const("ident_bf16")
    onesrow_b, bonesrow_b = ctx.load_const("ones_row_bf16")
    masks = {mk: ctx.load_const(mk) for mk in ("mask_first", "mask_mid", "mask_last")}

    for i, bn in enumerate(("bq", "bk", "bv", "bo")):
        emit_bias_row(ctx, W[bn], i, D, B)

    scale = 1.0 / np.sqrt(HD)
    for c in range(NCH):
        for wkey, dstinfo in (("Wq", ("qp", False)), ("Wk", ("kp", False)),
                              ("Wv", (None, True))):
            wg, bwg = emit_wt_granule_T(ctx, W[wkey], c, B)
            bias_idx = {"Wq": 0, "Wk": 1, "Wv": 2}[wkey]
            for tg in range(NTG):
                t0 = tg * 512
                fns = []
                for ic in range(NCH):
                    fns.append((lambda ic=ic, t0=t0: (
                        lambda e: e.matmul(psB[:, :], wg[:, ic * 128:(ic + 1) * 128],
                                           XN[:, ic * S + t0: ic * S + t0 + 512],
                                           start=(ic == 0), stop=False)))())
                fns.append((lambda c=c, bias_idx=bias_idx: (
                    lambda e: e.matmul(psB[:, :], brow[0:1, bias_idx * D + c * 128: bias_idx * D + (c + 1) * 128],
                                       onesrow_b[0:1, :], start=False, stop=True)))())
                p.group("pe", fns, reads=(bwg, bXN, bbrow, bonesrow_b), writes=(bpsB,))
                if not dstinfo[1]:
                    dst, bdst = B[dstinfo[0]]
                    p.op("dve", (lambda dst=dst, t0=t0: (
                        lambda e: e.tensor_copy(dst[:, t0:t0 + 512], psB[:, :])))(),
                        reads=(bpsB,), writes=(B[dstinfo[0]][1],))
                else:
                    p.op("dve", (lambda t0=t0: (
                        lambda e: e.tensor_copy(vstage[:, :], psB[:, :])))(),
                        reads=(bpsB,), writes=(bvstage,))
                    for j in range(4):
                        tt = tg * 4 + j
                        p.op("pe", (lambda j=j: (
                            lambda e: e.transpose(psTb[:, 0:128], vstage[:, j * 128:(j + 1) * 128], identb[:, :])))(),
                            reads=(bvstage, bidentb), writes=(bpsTb,))
                        p.op("dve", (lambda tt=tt: (
                            lambda e: e.tensor_copy(vtp[:, tt * 128:(tt + 1) * 128], psTb[:, 0:128])))(),
                            reads=(bpsTb,), writes=(bvtp,))
        for tt in range(NT):
            if tt == 0:
                k0, nkt, mkey = 0, 2, "mask_first"
            elif tt == NT - 1:
                k0, nkt, mkey = tt - 1, 2, "mask_last"
            else:
                k0, nkt, mkey = tt - 1, 3, "mask_mid"
            Wk_ = nkt * 128
            mten, mbuf = masks[mkey]
            for hh in range(2):
                rb = hh * 64
                p.group("pe", [(lambda rb=rb, tt=tt, k0=k0, Wk_=Wk_: (
                    lambda e: e.matmul(psS[:, 0:Wk_], qp[rb:rb + 64, tt * 128:(tt + 1) * 128],
                                       kp[rb:rb + 64, k0 * 128: k0 * 128 + Wk_],
                                       start=True, stop=True)))()],
                    reads=(bqp, bkp), writes=(bpsS,))
                softmax_tile(ctx, psS, bpsS, Wk_, mten[:, 0:Wk_], mbuf, scale, Pb, bPb, B)
                for j in range(nkt):
                    p.op("pe", (lambda j=j: (
                        lambda e: e.transpose(psTb[:, j * 128:(j + 1) * 128],
                                              Pb[:, j * 128:(j + 1) * 128], identb[:, :])))(),
                        reads=(bPb, bidentb), writes=(bpsTb,))
                p.op("dve", (lambda n=nkt * 128: (
                    lambda e: e.tensor_copy(PT[:, 0:n], psTb[:, 0:n])))(),
                    reads=(bpsTb,), writes=(bPT,))
                fns = []
                for j in range(nkt):
                    kt = k0 + j
                    fns.append((lambda kt=kt, rb=rb, j=j, first=(j == 0), last=(j == nkt - 1): (
                        lambda e: e.matmul(psC[0:64, 0:128], vtp[:, kt * 128 + rb: kt * 128 + rb + 64],
                                           PT[:, j * 128:(j + 1) * 128], start=first, stop=last)))())
                p.group("pe", fns, reads=(bvtp, bPT), writes=(bpsC,))
                p.op("dve", (lambda c=c, rb=rb, tt=tt: (
                    lambda e: e.tensor_copy(AO[rb:rb + 64, c * S + tt * 128: c * S + (tt + 1) * 128],
                                            psC[0:64, 0:128])))(),
                    reads=(bpsC,), writes=(bAO,))

    emit_out_proj_residual(ctx, B, W["Wo"], 3, "AO", transposed=True)



def alloc_common(ctx, smw=2048, smew=2048, need_brow=True, smedt=F32):
    B = {}
    B["X"] = ctx.sb("X", [128, NCH * S], F32)
    B["XN"] = ctx.sb("XN", [128, NCH * S], BF16)
    B["stage"] = ctx.sb("stage", [128, D], F32)
    B["t1"] = ctx.sb("t1", [128, 512], F32)
    B["sq"] = ctx.sb("sq", [128, 512], F32)
    B["rows"] = ctx.sb("rows", [1, 2048], F32)
    B["smS"] = ctx.sb("smS", [128, smw], F32)
    B["smE"] = ctx.sb("smE", [128, smew], smedt)
    B["smrows"] = ctx.sb("smrows", [128, 8], F32)
    B["wg"] = ctx.sb("wgran", [128, NCH * 128], BF16)
    B["wstage"] = B["stage"]
    if need_brow:
        B["brow"] = ctx.sb("brow", [1, 4 * D], BF16)
        if need_brow == "slim":
            B["rowstage"] = B["rows"]
        else:
            B["rowstage"] = ctx.sb("rowstage", [1, D], F32)
    B["psS"] = ctx.ps("psS", [128, 2048], F32)
    B["psSr3"] = (B["psS"][0], Buf("psSr3"))
    B["psA"] = ctx.ps("psA", [128, 512], F32)
    B["psB"] = ctx.ps("psB", [128, 512], F32)
    B["psC"] = ctx.ps("psC", [128, 512], F32)
    B["psTbf"] = ctx.ps("psTbf", [128, 512], BF16)
    B["psT32"] = (B["psC"][0], B["psC"][1])
    return B


def build_l0_attn_a_program():
    consts = host_constants()
    nc = bass.Bass("TRN2", target_bir_lowering=False)
    caps = declare_constants(nc, consts)
    x_in = nc.dram_tensor("x_in", [2, S, D], F32, kind="ExternalInput").ap()
    x_out = nc.dram_tensor("x_out", [2, S, D], F32, kind="ExternalOutput").ap()
    Wd = {}
    for nm, shape in (("Wq", [D, D]), ("bq", [D]), ("Wk", [D, D]), ("bk", [D]),
                      ("Wv", [D, D]), ("bv", [D]), ("Wo", [D, D]), ("bo", [D])):
        Wd[nm] = nc.dram_tensor("w_" + nm, shape, F32, kind="ExternalInput").ap()
    ctx = Ctx(nc, caps)
    B = alloc_common(ctx, smw=512, smew=512)
    B["qp"] = ctx.sb("qp", [128, S], BF16)
    B["kp"] = ctx.sb("kp", [128, S], BF16)
    B["vtp"] = ctx.sb("vtp", [128, NT * 128], BF16)
    B["vstage"] = ctx.sb("vstage", [128, 512], BF16)
    B["AO"] = ctx.sb("AO", [128, NCH * S], BF16)
    B["PT"] = ctx.sb("PTb", [128, 3 * 128], BF16)
    B["Pbuf"] = ctx.sb("Pbuf", [128, 384], BF16)
    dbg_aps = {}
    if DBG.get("on"):
        for nm, shape, dt in (("dX", [128, NCH * S], F32), ("dXN", [128, NCH * S], BF16),
                              ("dqp", [128, S], BF16), ("dkp", [128, S], BF16),
                              ("dvtp", [128, NT * 128], BF16), ("dAO", [128, NCH * S], BF16)):
            dbg_aps[nm] = nc.dram_tensor(nm, shape, dt, kind="ExternalOutput").ap()
    for seq in range(2):
        emit_load_x(ctx, x_in[seq], B["X"][0], B["X"][1], B["stage"][0], B["stage"][1],
                    B["psT32"][0], B["psT32"][1])
        if seq == 0 and dbg_aps:
            ctx.p.dma("pool", dbg_aps["dX"][:, :], B["X"][0][:, :], reads=(B["X"][1],), stream="dbg")
        emit_ln(ctx, B)
        if seq == 0 and dbg_aps:
            ctx.p.dma("pool", dbg_aps["dXN"][:, :], B["XN"][0][:, :], reads=(B["XN"][1],), stream="dbg")
        emit_l0_attn_a(ctx, B, Wd)
        if seq == 0 and dbg_aps:
            for nm, key in (("dqp", "qp"), ("dkp", "kp"), ("dvtp", "vtp"), ("dAO", "AO")):
                ctx.p.dma("pool", dbg_aps[nm][:, :], B[key][0][:, :], reads=(B[key][1],), stream="dbg")
        emit_store_x(ctx, B["X"][0], B["X"][1], x_out[seq], B["stage"][0], B["stage"][1],
                     B["psT32"][0], B["psT32"][1])
    ctx.p.drain("sp", [B["stage"][1]])
    ctx.p.replay(nc)
    return nc, consts



def emit_attn_core(ctx, B, qtile_ap, k_ap, bk, vtile, bv, write_out, scale,
                   causal, use_softmax=True, bq_src=None):
    p = ctx.p
    Pb, bPb = B["Pbuf"]
    PT, bPT = B["PT"]
    psS, bpsS = B["psS"]
    psA, bpsA = B["psA"]
    psTb, bpsTb = B["psTbf"]
    identb, bidentb = ctx.load_const("ident_bf16")
    if causal and use_softmax:
        mtril, bmtril = ctx.load_const("mask_tril")
    if not use_softmax:
        mstril, bmstril = ctx.load_const("mask_stril_f32")
    for qg in range(4):
        ktmax = 4 * qg + 4 if causal else NT
        if causal:
            p.op("dve", (lambda n=ktmax * 512: (
                lambda e: e.memset(PT[:, 0:n], 0.0)))(), writes=(bPT,))
        for sub in range(4):
            tt = qg * 4 + sub
            Wd = (tt + 1) * 128 if causal else S
            fns = []
            for kg0 in range(0, Wd, 512):
                kw = min(512, Wd - kg0)
                fns.append((lambda tt=tt, kg0=kg0, kw=kw: (
                    lambda e: e.matmul(psS[:, kg0:kg0 + kw], qtile_ap(tt),
                                       k_ap[:, kg0:kg0 + kw],
                                       start=True, stop=True)))())
            p.group("pe", fns, reads=(bq_src, bk), writes=(bpsS,))
            nkt = Wd // 128
            if use_softmax:
                if causal:
                    p.op("dve", (lambda Wd=Wd: (
                        lambda e: e.tensor_add(psS[:, Wd - 128:Wd], psS[:, Wd - 128:Wd],
                                               mtril[:, :])))(),
                        reads=(bmtril,), writes=(bpsS,))
                softmax_tile(ctx, psS, bpsS, Wd, None, None, scale, Pb, bPb, B)
            else:
                if Wd > 128:
                    p.op("dve", (lambda Wd=Wd: (
                        lambda e: e.tensor_scalar_mul(Pb[:, 0:Wd - 128], psS[:, 0:Wd - 128],
                                                      float(scale))))(),
                        reads=(bpsS,), writes=(bPb,))
                p.op("dve", (lambda Wd=Wd: (
                    lambda e: e.scalar_tensor_tensor(Pb[:, Wd - 128:Wd], psS[:, Wd - 128:Wd],
                                                     float(scale), mstril[:, :],
                                                     ALU.mult, ALU.mult)))(),
                    reads=(bpsS, bmstril), writes=(bPb,))
            for j0 in range(0, nkt, 4):
                jn = min(4, nkt - j0)
                for j in range(jn):
                    p.op("pe", (lambda j0=j0, j=j: (
                        lambda e: e.transpose(psTb[:, j * 128:(j + 1) * 128],
                                              Pb[:, (j0 + j) * 128:(j0 + j + 1) * 128],
                                              identb[:, :])))(),
                        reads=(bPb, bidentb), writes=(bpsTb,))
                for j in range(jn):
                    p.op("dve", (lambda j0=j0, j=j, sub=sub: (
                        lambda e: e.tensor_copy(PT[:, (j0 + j) * 512 + sub * 128:
                                                   (j0 + j) * 512 + sub * 128 + 128],
                                                psTb[:, j * 128:(j + 1) * 128])))(),
                        reads=(bpsTb,), writes=(bPT,))
        fns = []
        for kt in range(ktmax):
            fns.append((lambda kt=kt, first=(kt == 0), last=(kt == ktmax - 1): (
                lambda e: e.matmul(psA[0:64, :], vtile(kt), PT[:, kt * 512:(kt + 1) * 512],
                                   start=first, stop=last)))())
        p.group("pe", fns, reads=(bv, bPT), writes=(bpsA,))
        write_out(qg, psA, bpsA)


def emit_pair_proj(ctx, B, w_dram, dst_pair, oc, src_pair, in_dim=D,
                   transposed=True, bias_slice=None, to_tok=None, dst_off=0,
                   evac=None):
    p = ctx.p
    src, bsrc = src_pair
    psB, bpsB = B["psB"]
    psTb, bpsTb = B["psTbf"]
    vstage, bvstage = B.get("vstage", (None, None))
    identb, bidentb = ctx.load_const("ident_bf16")
    onesrow_b, bonesrow_b = ctx.load_const("ones_row_bf16")
    if transposed:
        wg, bwg = emit_wt_granule_T(ctx, w_dram, oc, B, in_dim=in_dim)
    else:
        wg, bwg = emit_w_granule(ctx, w_dram, oc, B, in_dim=in_dim)
    nic = in_dim // 128
    brow_b = B["brow"][1] if "brow" in B else bwg
    for tg in range(NTG):
        t0 = tg * 512
        fns = []
        for ic in range(nic):
            fns.append((lambda ic=ic, t0=t0, last=(bias_slice is None and ic == nic - 1): (
                lambda e: e.matmul(psB[:, :], wg[:, ic * 128:(ic + 1) * 128],
                                   src[:, ic * S + t0: ic * S + t0 + 512],
                                   start=(ic == 0), stop=last)))())
        if bias_slice is not None:
            fns.append((lambda: (
                lambda e: e.matmul(psB[:, :], bias_slice, onesrow_b[0:1, :],
                                   start=False, stop=True)))())
        p.group("pe", fns, reads=(bwg, bsrc, brow_b, bonesrow_b), writes=(bpsB,))
        if dst_pair is not None:
            dst, bdst = dst_pair
            if evac is None:
                p.op("dve", (lambda t0=t0, dst_off=dst_off: (
                    lambda e: e.tensor_copy(dst[:, dst_off + t0: dst_off + t0 + 512], psB[:, :])))(),
                    reads=(bpsB,), writes=(bdst,))
            else:
                func, scl = evac
                p.op("act", (lambda t0=t0, dst_off=dst_off, func=func, scl=scl: (
                    lambda e: e.activation(dst[:, dst_off + t0: dst_off + t0 + 512], psB[:, :],
                                           func, scale=float(scl))))(),
                    reads=(bpsB,), writes=(bdst,))
        if to_tok is not None:
            vt, bvt = to_tok
            if dst_pair is not None:
                tsrc = dst[:, dst_off + t0: dst_off + t0 + 512]
                btsrc = dst_pair[1]
            else:
                p.op("dve", (lambda: (
                    lambda e: e.tensor_copy(vstage[:, :], psB[:, :])))(),
                    reads=(bpsB,), writes=(bvstage,))
                tsrc = vstage[:, 0:512]
                btsrc = bvstage
            for j in range(4):
                tt = tg * 4 + j
                p.op("pe", (lambda j=j, tsrc=tsrc: (
                    lambda e: e.transpose(psTb[:, 0:128], tsrc[:, j * 128:(j + 1) * 128],
                                          identb[:, :])))(),
                    reads=(btsrc, bidentb), writes=(bpsTb,))
                p.op("dve", (lambda tt=tt: (
                    lambda e: e.tensor_copy(vt[:, tt * 128:(tt + 1) * 128], psTb[:, 0:128])))(),
                    reads=(bpsTb,), writes=(bvt,))



def emit_l0_attn_b(ctx, B, W):
    p = ctx.p
    X, bX = B["X"]; XN, bXN = B["XN"]
    AO, bAO = B["AO"]
    lat, blat = B["lat"]
    qp, bqp = B["qp"]; kp, bkp = B["kp"]
    vtp, bvtp = B["vtp"]
    scale = 1.0 / np.sqrt(HD)
    wq2d = W["Wq"].rearrange("d h k -> d (h k)")
    wku = W["Wk_up"].rearrange("c h k -> c (h k)")
    wvu = W["Wv_up"].rearrange("c h k -> c (h k)")
    wout = W["Wout"].rearrange("h k o -> (h k) o")
    for oc in range(2):
        emit_pair_proj(ctx, B, W["Wkv_down"], (lat, blat), oc, (XN, bXN),
                       transposed=False, dst_off=oc * S)
    for c in range(NCH):
        emit_pair_proj(ctx, B, wq2d, (qp, bqp), c, (XN, bXN), transposed=False)
        emit_pair_proj(ctx, B, wku, (kp, bkp), c, (lat, blat), in_dim=256,
                       transposed=False)
        emit_pair_proj(ctx, B, wvu, None, c, (lat, blat), in_dim=256,
                       transposed=False, to_tok=(vtp, bvtp))
        for hh in range(2):
            rb = hh * 64
            qtile_ap = (lambda rb=rb: (lambda tt: qp[rb:rb + 64, tt * 128:(tt + 1) * 128]))()
            vtile = (lambda rb=rb: (lambda kt: vtp[:, kt * 128 + rb: kt * 128 + rb + 64]))()
            def write_out(qg, ps, bps, c=c, rb=rb):
                p.op("dve", (lambda qg=qg, c=c, rb=rb, ps=ps: (
                    lambda e: e.tensor_copy(AO[rb:rb + 64, c * S + qg * 512: c * S + qg * 512 + 512],
                                            ps[0:64, :])))(),
                    reads=(bps,), writes=(bAO,))
            emit_attn_core(ctx, B, qtile_ap, kp[rb:rb + 64, :], bkp, vtile, bvtp,
                           write_out, scale, causal=True, use_softmax=True,
                           bq_src=bqp)
    emit_out_proj_residual(ctx, B, wout, None, "AO", transposed=False)



def emit_ffn(ctx, B, W, kind):
    p = ctx.p
    X, bX = B["X"]; XN, bXN = B["XN"]
    hg, bhg = B["hg"]
    gates, bgates = B["gates"]
    GMT, bGMT = B["GMT"]
    grow, bgrow = B["grow"]
    oacc, boacc = B["oacc"]
    sq, bsq = B["sq"]; t1, bt1 = B["t1"]
    rows, brows = B["rows"]
    srows, bsrows = B["smrows"]
    psS, bpsS = B["psS"]; psA, bpsA = B["psA"]
    psB, bpsB = B["psB"]; psC, bpsC = B["psC"]
    bpsU = (Buf("psS_u0"), Buf("psS_u1"))
    bpsG = Buf("psS_gate")
    gwT, bgwT = B["gwT"]
    stage, bstage = B["wstage"]
    ident, bident = ctx.load_const("ident_f32")
    onesrow, bonesrow = ctx.load_const("ones_row_f32")
    onescol, bonescol = ctx.load_const("ones_col")
    onesrow_b, bonesrow_b = ctx.load_const("ones_row_bf16")
    E = 8
    NH = 4096 // 128
    NEGBIG = -1.0e30

    if kind == "l1":
        rwf, brwf = B["rwf"]
        p.dma("pool", stage[0:NCH, 0:128], W["rms_w"].rearrange("(c p) -> c p", p=128),
              writes=(bstage,), stream="bias")
        p.op("pe", (lambda: (
            lambda e: e.transpose(psC[:, 0:NCH], stage[0:NCH, 0:128], ident[0:NCH, 0:NCH])))(),
            reads=(bstage, bident), writes=(bpsC,))
        p.op("dve", (lambda: (lambda e: e.tensor_copy(rwf[:, :], psC[:, 0:NCH])))(),
             reads=(bpsC,), writes=(brwf,))
        epsc, bepsc = ctx.constcell(1e-6)
        for tg in range(NTG):
            t0 = tg * 512
            for c in range(NCH):
                p.op("act", (lambda c=c, t0=t0: (
                    lambda e: e.activation(sq[:, :], XN[:, c * S + t0: c * S + t0 + 512], AF.Square)))(),
                    reads=(bXN,), writes=(bsq,))
                p.group("pe", [(lambda c=c: (
                    lambda e: e.matmul(psB[0:1, :], onescol[:, :], sq[:, :],
                                       start=(c == 0), stop=(c == NCH - 1))))()],
                    reads=(bsq, bonescol), writes=(bpsB,))
            ms = rows[0:1, 0:512]; sr = rows[0:1, 512:1024]; rr = rows[0:1, 1024:1536]
            p.op("dve", (lambda ms=ms: (lambda e: e.tensor_scalar_mul(ms, psB[0:1, :], 1.0 / D)))(),
                 reads=(bpsB,), writes=(brows,))
            p.op("act", (lambda sr=sr, ms=ms: (
                lambda e: e.activation(sr, ms, AF.Sqrt, bias=epsc[0:1, 0:1])))(),
                reads=(brows, bepsc), writes=(brows,))
            p.op("dve", (lambda rr=rr, sr=sr: (lambda e: e.reciprocal(rr, sr)))(),
                 reads=(brows,), writes=(brows,))
            p.group("pe", [(lambda rr=rr: (
                lambda e: e.matmul(psA[:, 0:512], onesrow[:, :], rr, start=True, stop=True)))()],
                reads=(brows, bonesrow), writes=(bpsA,))
            for c in range(NCH):
                p.op("dve", (lambda c=c, t0=t0: (
                    lambda e: e.tensor_mul(t1[:, :], XN[:, c * S + t0: c * S + t0 + 512], psA[:, 0:512])))(),
                    reads=(bXN, bpsA), writes=(bt1,))
                p.op("dve", (lambda c=c, t0=t0: (
                    lambda e: e.tensor_scalar_mul(XN[:, c * S + t0: c * S + t0 + 512], t1[:, :],
                                                  rwf[:, c:c + 1])))(),
                    reads=(bt1, brwf), writes=(bXN,))

    if kind == "l0":
        p.dma("pool", stage[:, 0:NCH * 8].rearrange("p (c o) -> p c o", o=8),
              W["router_w"].rearrange("(c p) o -> p c o", p=128),
              writes=(bstage,), stream="bias")
        p.op("dve", (lambda: (lambda e: e.tensor_copy(gwT[:, :], stage[:, 0:NCH * 8])))(),
             reads=(bstage,), writes=(bgwT,))
    else:
        p.dma("pool", stage[0:8, 0:D], W["gate_w"][:, :], writes=(bstage,), stream="bias")
        for c in range(NCH):
            p.op("pe", (lambda c=c: (
                lambda e: e.transpose(psC[:, 0:8], stage[0:8, c * 128:(c + 1) * 128],
                                      ident[0:8, 0:8])))(),
                reads=(bstage, bident), writes=(bpsC,))
            p.op("dve", (lambda c=c: (
                lambda e: e.tensor_copy(gwT[:, c * 8:(c + 1) * 8], psC[:, 0:8])))(),
                reads=(bpsC,), writes=(bgwT,))

    for tt in range(NT):
        fns = []
        for c in range(NCH):
            fns.append((lambda c=c, tt=tt: (
                lambda e: e.matmul(psC[:, 0:8], XN[:, c * S + tt * 128: c * S + (tt + 1) * 128],
                                   gwT[:, c * 8:(c + 1) * 8],
                                   start=(c == 0), stop=(c == NCH - 1))))())
        p.group("pe", fns, reads=(bXN, bgwT), writes=(bpsC,))
        g0 = tt * 8
        p.op("dve", (lambda: (
            lambda e: e.reduce_max(srows[:, 0:1], psC[:, 0:8], AX.X, negate=True)))(),
            reads=(bpsC,), writes=(bsrows,))
        p.op("act", (lambda: (
            lambda e: e.activation(t1[:, 0:8], psC[:, 0:8], AF.Exp, bias=srows[:, 0:1])))(),
            reads=(bpsC, bsrows), writes=(bt1,))
        p.op("dve", (lambda: (
            lambda e: e.reduce_sum(srows[:, 1:2], t1[:, 0:8], AX.X)))(),
            reads=(bt1,), writes=(bsrows,))
        p.op("dve", (lambda: (
            lambda e: e.reciprocal(srows[:, 2:3], srows[:, 1:2])))(),
            reads=(bsrows,), writes=(bsrows,))
        p.op("dve", (lambda: (
            lambda e: e.tensor_scalar_mul(t1[:, 16:24], t1[:, 0:8], srows[:, 2:3])))(),
            reads=(bt1, bsrows), writes=(bt1,))
        if kind == "l1":
            p.op("dve", (lambda: (
                lambda e: e.reduce_max(srows[:, 3:4], t1[:, 16:24], AX.X)))(),
                reads=(bt1,), writes=(bsrows,))
            p.op("dve", (lambda: (
                lambda e: e.tensor_single_scalar(t1[:, 24:32], t1[:, 16:24], srows[:, 3:4], ALU.is_ge)))(),
                reads=(bt1, bsrows), writes=(bt1,))
            p.op("dve", (lambda g0=g0: (
                lambda e: e.tensor_scalar_mul(gates[:, g0:g0 + 8], t1[:, 24:32], srows[:, 3:4])))(),
                reads=(bt1, bsrows), writes=(bgates,))
        else:
            p.op("dve", (lambda: (
                lambda e: e.tensor_copy(t1[:, 24:32], t1[:, 16:24])))(),
                reads=(bt1,), writes=(bt1,))
            for j in range(3):
                p.op("dve", (lambda: (
                    lambda e: e.reduce_max(srows[:, 3:4], t1[:, 24:32], AX.X)))(),
                    reads=(bt1,), writes=(bsrows,))
                p.op("dve", (lambda: (
                    lambda e: e.tensor_single_scalar(t1[:, 8:16], t1[:, 24:32], srows[:, 3:4], ALU.is_ge)))(),
                    reads=(bt1, bsrows), writes=(bt1,))
                p.op("dve", (lambda: (
                    lambda e: e.scalar_tensor_tensor(t1[:, 24:32], t1[:, 8:16], NEGBIG, t1[:, 24:32],
                                                     ALU.mult, ALU.add)))(),
                    reads=(bt1,), writes=(bt1,))
            p.op("dve", (lambda: (
                lambda e: e.reduce_max(srows[:, 3:4], t1[:, 24:32], AX.X)))(),
                reads=(bt1,), writes=(bsrows,))
            p.op("dve", (lambda: (
                lambda e: e.tensor_single_scalar(t1[:, 8:16], t1[:, 16:24], srows[:, 3:4], ALU.is_ge)))(),
                reads=(bt1, bsrows), writes=(bt1,))
            p.op("dve", (lambda: (
                lambda e: e.tensor_mul(t1[:, 24:32], t1[:, 16:24], t1[:, 8:16])))(),
                reads=(bt1,), writes=(bt1,))
            p.op("dve", (lambda: (
                lambda e: e.reduce_sum(srows[:, 4:5], t1[:, 24:32], AX.X)))(),
                reads=(bt1,), writes=(bsrows,))
            p.op("dve", (lambda: (
                lambda e: e.reciprocal(srows[:, 5:6], srows[:, 4:5])))(),
                reads=(bsrows,), writes=(bsrows,))
            p.op("dve", (lambda g0=g0: (
                lambda e: e.tensor_scalar_mul(gates[:, g0:g0 + 8], t1[:, 24:32], srows[:, 5:6])))(),
                reads=(bt1, bsrows), writes=(bgates,))
        p.op("pe", (lambda g0=g0: (
            lambda e: e.transpose(psC[0:8, 0:128], gates[:, g0:g0 + 8], ident[:, :])))(),
            reads=(bgates, bident), writes=(bpsC,))
        p.op("dve", (lambda tt=tt: (
            lambda e: e.tensor_copy(GMT[0:8, tt * 128:(tt + 1) * 128], psC[0:8, 0:128])))(),
            reads=(bpsC,), writes=(bGMT,))

    if kind == "l1":
        b1f, bb1f = B["b1f"]
        b2b, bb2b = B["b2b"]
        stage2, bstage2 = B["wstage2"]
        for e_ in range(E):
            p.dma("pool", stage[0:NH, 0:128],
                  W["b1"][e_].rearrange("(c p) -> c p", p=128), writes=(bstage,), stream="bias")
            p.op("pe", (lambda: (
                lambda e: e.transpose(psC[:, 0:NH], stage[0:NH, 0:128], ident[0:NH, 0:NH])))(),
                reads=(bstage, bident), writes=(bpsC,))
            p.op("dve", (lambda e_=e_: (
                lambda e: e.tensor_copy(b1f[:, e_ * NH:(e_ + 1) * NH], psC[:, 0:NH])))(),
                reads=(bpsC,), writes=(bb1f,))
        p.dma("pool", stage2[0:8, 0:D], W["b2"][:, :], writes=(bstage2,), stream="bias")
        p.op("dve", (lambda: (lambda e: e.tensor_copy(b2b[0:8, 0:D], stage2[0:8, 0:D])))(),
             reads=(bstage2,), writes=(bb2b,))

    w1s = [W["up_w"][e_] if kind == "l0" else W["W1"][e_] for e_ in range(E)]
    w2s = [W["down_w"][e_] if kind == "l0" else W["W2"][e_] for e_ in range(E)]
    for tg in range(NTG):
        t0 = tg * 512
        for dc in range(NCH):
            p.op("dve", (lambda dc=dc: (
                lambda e: e.memset(oacc[:, dc * 512:(dc + 1) * 512], 0.0)))(),
                writes=(boacc,))
        for e_ in range(E):
            p.dma("sp", grow[0:1, 0:S], GMT[0:1, 0:S].tensor[e_:e_ + 1, 0:S]
                  if False else GMT[e_:e_ + 1, 0:S],
                  reads=(bGMT,), writes=(bgrow,), stream="grow")
            p.group("pe", [(lambda t0=t0: (
                lambda e: e.matmul(psS[:, 1024:1536], onesrow_b[0:1, 0:128],
                                   grow[0:1, t0:t0 + 512], start=True, stop=True)))()],
                reads=(bgrow, bonesrow_b), writes=(bpsG,))
            for hc in range(NH):
                ping = hc % 2
                wg, bwg = emit_w_granule(ctx, w1s[e_], hc, B, in_dim=D,
                                         bufkey="wg" if ping == 0 else "wgB")
                fns = []
                for ic in range(NCH):
                    fns.append((lambda ic=ic, t0=t0, ping=ping, wg=wg: (
                        lambda e: e.matmul(psS[:, ping * 512:(ping + 1) * 512],
                                           wg[:, ic * 128:(ic + 1) * 128],
                                           XN[:, ic * S + t0: ic * S + t0 + 512],
                                           start=(ic == 0), stop=(ic == NCH - 1))))())
                p.group("pe", fns, reads=(bwg, bXN), writes=(bpsU[ping],))
                tpp = B["t1"] if ping == 0 else B["sq"]
                if kind == "l0":
                    p.op("act", (lambda ping=ping, tp=tpp[0]: (
                        lambda e: e.activation(tp[:, :], psS[:, ping * 512:(ping + 1) * 512],
                                               AF.Gelu_apprx_tanh)))(),
                        reads=(bpsU[ping],), writes=(tpp[1],))
                else:
                    p.op("act", (lambda ping=ping, e_=e_, hc=hc, tp=tpp[0]: (
                        lambda e: e.activation(tp[:, :], psS[:, ping * 512:(ping + 1) * 512],
                                               AF.Silu,
                                               bias=B["b1f"][0][:, e_ * NH + hc: e_ * NH + hc + 1])))(),
                        reads=(bpsU[ping], B["b1f"][1]), writes=(tpp[1],))
                p.op("dve", (lambda hc=hc, tp=tpp[0]: (
                    lambda e: e.tensor_mul(hg[:, hc * 512:(hc + 1) * 512], tp[:, :],
                                           psS[:, 1024:1536])))(),
                    reads=(tpp[1], bpsG), writes=(bhg,))
            for dc in range(NCH):
                pbank, bpbank = (psA, bpsA) if dc % 2 == 0 else (psB, bpsB)
                wg2, bwg2 = emit_w_granule(ctx, w2s[e_], dc, B, in_dim=4096,
                                           bufkey="wg2" if dc % 2 == 0 else "wg2B")
                fns = []
                for hc in range(NH):
                    last = (hc == NH - 1) and not (kind == "l1" and e_ == E - 1)
                    fns.append((lambda hc=hc, pbank=pbank, last=last, wg2=wg2: (
                        lambda e: e.matmul(pbank[:, :], wg2[:, hc * 128:(hc + 1) * 128],
                                           hg[:, hc * 512:(hc + 1) * 512],
                                           start=(hc == 0), stop=last)))())
                reads = [bwg2, bhg]
                if kind == "l1" and e_ == E - 1:
                    fns.append((lambda dc=dc, t0=t0, pbank=pbank: (
                        lambda e: e.matmul(pbank[:, :], B["b2b"][0][0:8, dc * 128:(dc + 1) * 128],
                                           GMT[0:8, t0:t0 + 512], start=False, stop=True)))())
                    reads += [B["b2b"][1], bGMT]
                p.group("pe", fns, reads=tuple(reads), writes=(bpbank,))
                p.op("dve", (lambda dc=dc, pbank=pbank: (
                    lambda e: e.tensor_add(oacc[:, dc * 512:(dc + 1) * 512],
                                           oacc[:, dc * 512:(dc + 1) * 512], pbank[:, :])))(),
                    reads=(bpbank,), writes=(boacc,))
        for dc in range(NCH):
            p.op("dve", (lambda dc=dc, t0=t0: (
                lambda e: e.tensor_add(X[:, dc * S + t0: dc * S + t0 + 512],
                                       X[:, dc * S + t0: dc * S + t0 + 512],
                                       oacc[:, dc * 512:(dc + 1) * 512])))(),
                reads=(boacc,), writes=(bX,))



def build_common(weights_spec, extra_alloc, emit_fn, smw=128, smew=2048,
                 need_brow=True, smedt=F32):
    consts = host_constants()
    nc = bass.Bass("TRN2", target_bir_lowering=False)
    caps = declare_constants(nc, consts)
    x_in = nc.dram_tensor("x_in", [2, S, D], F32, kind="ExternalInput").ap()
    x_out = nc.dram_tensor("x_out", [2, S, D], F32, kind="ExternalOutput").ap()
    Wd = {}
    for nm, shape in weights_spec:
        Wd[nm] = nc.dram_tensor("w_" + nm, shape, F32, kind="ExternalInput").ap()
    ctx = Ctx(nc, caps)
    B = alloc_common(ctx, smw=smw, smew=smew, need_brow=need_brow, smedt=smedt)
    extra_alloc(ctx, B)
    for seq in range(2):
        emit_load_x(ctx, x_in[seq], B["X"][0], B["X"][1], B["stage"][0], B["stage"][1],
                    B["psT32"][0], B["psT32"][1])
        emit_ln(ctx, B)
        emit_fn(ctx, B, Wd)
        emit_store_x(ctx, B["X"][0], B["X"][1], x_out[seq], B["stage"][0], B["stage"][1],
                     B["psT32"][0], B["psT32"][1])
    ctx.p.drain("sp", [B["stage"][1]])
    ctx.p.replay(nc)
    return nc, consts


def alloc_l0_attn_b(ctx, B):
    B["lat"] = ctx.sb("lat", [128, 2 * S], BF16)
    B["qp"] = ctx.sb("qp", [128, S], BF16)
    B["kp"] = ctx.sb("kp", [128, S], BF16)
    B["vtp"] = ctx.sb("vtp", [128, NT * 128], BF16)
    B["vstage"] = ctx.sb("vstage", [128, 512], BF16)
    B["AO"] = ctx.sb("AO", [128, NCH * S], BF16)
    B["PT"] = ctx.sb("PTb", [128, NT * 512], BF16)
    B["Pbuf"] = ctx.sb("Pbuf", [128, S], BF16)


def build_l0_attn_b_program():
    spec = (("Wq", [D, H, HD]), ("Wkv_down", [D, 256]), ("Wk_up", [256, H, HD]),
            ("Wv_up", [256, H, HD]), ("Wout", [H, HD, D]))
    return build_common(spec, alloc_l0_attn_b, emit_l0_attn_b, smw=8,
                        need_brow=False)


def alloc_ffn(ctx, B):
    B["hg"] = ctx.sb("hgbuf", [128, 32 * 512], BF16)
    B["oacc"] = ctx.sb("oacc", [128, 8 * 512], F32)
    B["gates"] = ctx.sb("gates", [128, NT * 8], F32)
    B["GMT"] = ctx.sb("GMT", [128, S], BF16)
    B["grow"] = ctx.sb("grow", [1, S], BF16)
    B["gwT"] = ctx.sb("gwT", [128, NCH * 8], BF16)
    B["wg2"] = ctx.sb("wg2", [128, 4096], BF16)
    B["wstage2"] = ctx.sb("wstage2", [128, 1024], F32)
    B["wgB"] = ctx.sb("wgB", [128, NCH * 128], BF16)
    B["wstageB"] = ctx.sb("wstageB", [128, D], F32)
    B["wg2B"] = ctx.sb("wg2B", [128, 4096], BF16)
    B["wstage2B"] = ctx.sb("wstage2B", [128, 1024], F32)
    B["rwf"] = ctx.sb("rwf", [128, NCH], F32)
    B["b1f"] = ctx.sb("b1f", [128, 8 * 32], F32)
    B["b2b"] = ctx.sb("b2b", [128, D], BF16)


def build_l0_ffn_program():
    spec = (("router_w", [D, 8]), ("up_w", [8, D, 4096]), ("down_w", [8, 4096, D]))
    return build_common(spec, alloc_ffn, lambda ctx, B, W: emit_ffn(ctx, B, W, "l0"),
                        smw=8, smew=8, need_brow=False)


def build_l1_ffn_program():
    spec = (("rms_w", [D]), ("gate_w", [8, D]), ("W1", [8, D, 4096]),
            ("b1", [8, 4096]), ("W2", [8, 4096, D]), ("b2", [8, D]))
    return build_common(spec, alloc_ffn, lambda ctx, B, W: emit_ffn(ctx, B, W, "l1"),
                        smw=8, smew=8, need_brow=False)



def emit_l1_attn_a(ctx, B, W):
    p = ctx.p
    X, bX = B["X"]; XN, bXN = B["XN"]
    AO, bAO = B["AO"]
    qp, bqp = B["qp"]; kp, bkp = B["kp"]
    vf, bvf = B["vf"]; vtp, bvtp = B["vtp"]
    y12, by12 = B["y12"]; y22, by22 = B["y22"]; y1t, by1t = B["y1tok"]
    wsc, bwsc = B["wsc"]; wbc, bwbc = B["wbc"]
    psC, bpsC = B["psC"]
    identb, bidentb = ctx.load_const("ident_bf16")
    onesrow, bonesrow = ctx.load_const("ones_row_f32")
    scale = 1.0 / np.sqrt(HD)
    for i, nm in enumerate(("w0", "w1", "wK")):
        p.dma("pool", wsc[0:1, i * 16:(i + 1) * 16],
              W[nm].rearrange("(o n) -> o n", o=1), writes=(bwsc,), stream="bias")
    p.op("dve", (lambda: (
        lambda e: e.tensor_sub(wsc[0:1, 48:64], wsc[0:1, 16:32], wsc[0:1, 32:48])))(),
        reads=(bwsc,), writes=(bwsc,))
    p.op("dve", (lambda: (
        lambda e: e.tensor_scalar_mul(wsc[0:1, 64:80], wsc[0:1, 32:48], 2.0)))(),
        reads=(bwsc,), writes=(bwsc,))
    for i, s0 in enumerate((0, 48, 64)):
        p.group("pe", [(lambda s0=s0: (
            lambda e: e.matmul(psC[:, 0:16], onesrow[:, :], wsc[0:1, s0:s0 + 16],
                               start=True, stop=True)))()],
            reads=(bwsc, bonesrow), writes=(bpsC,))
        p.op("dve", (lambda i=i: (
            lambda e: e.tensor_copy(wbc[:, i * 16:(i + 1) * 16], psC[:, 0:16])))(),
            reads=(bpsC,), writes=(bwbc,))

    for i, bn in enumerate(("bq", "bk", "bv", "bo")):
        emit_bias_row(ctx, W[bn], i, D, B)
    brow, bbrow = B["brow"]

    for c in range(NCH):
        emit_pair_proj(ctx, B, W["Wq"], (qp, bqp), c, (XN, bXN), transposed=False,
                       bias_slice=brow[0:1, 0 * D + c * 128: 0 * D + (c + 1) * 128])
        emit_pair_proj(ctx, B, W["Wk"], (kp, bkp), c, (XN, bXN), transposed=False,
                       bias_slice=brow[0:1, 1 * D + c * 128: 1 * D + (c + 1) * 128])
        emit_pair_proj(ctx, B, W["Wv"], (vf, bvf), c, (XN, bXN), transposed=False,
                       bias_slice=brow[0:1, 2 * D + c * 128: 2 * D + (c + 1) * 128],
                       to_tok=(vtp, bvtp))
        for hh in range(2):
            rb = hh * 64
            h = 2 * c + hh
            qtile_ap = (lambda rb=rb: (lambda tt: qp[rb:rb + 64, tt * 128:(tt + 1) * 128]))()
            vtile1 = (lambda rb=rb: (lambda kt: vtp[:, kt * 128 + rb: kt * 128 + rb + 64]))()
            def wo1(qg, ps, bps, rb=rb):
                p.op("dve", (lambda qg=qg, ps=ps, rb=rb: (
                    lambda e: e.tensor_copy(y12[rb:rb + 64, qg * 512:(qg + 1) * 512], ps[0:64, :])))(),
                    reads=(bps,), writes=(by12,))
            emit_attn_core(ctx, B, qtile_ap, kp[rb:rb + 64, :], bkp, vtile1, bvtp,
                           wo1, scale, causal=False, use_softmax=True, bq_src=bqp)
            psTb, bpsTb = B["psTbf"]
            for tt in range(NT):
                p.op("pe", (lambda tt=tt, rb=rb: (
                    lambda e: e.transpose(psTb[:, 0:64], y12[rb:rb + 64, tt * 128:(tt + 1) * 128],
                                          identb[rb:rb + 64, rb:rb + 64])))(),
                    reads=(by12, bidentb), writes=(bpsTb,))
                p.op("dve", (lambda tt=tt: (
                    lambda e: e.tensor_copy(y1t[:, tt * 64:(tt + 1) * 64], psTb[:, 0:64])))(),
                    reads=(bpsTb,), writes=(by1t,))
            vtile2 = (lambda: (lambda kt: y1t[:, kt * 64:(kt + 1) * 64]))()
            def wo2(qg, ps, bps, rb=rb):
                p.op("dve", (lambda qg=qg, ps=ps, rb=rb: (
                    lambda e: e.tensor_copy(y22[rb:rb + 64, qg * 512:(qg + 1) * 512], ps[0:64, :])))(),
                    reads=(bps,), writes=(by22,))
            emit_attn_core(ctx, B, qtile_ap, kp[rb:rb + 64, :], bkp, vtile2, by1t,
                           wo2, scale, causal=False, use_softmax=True, bq_src=bqp)
            for tg in range(NTG):
                t0 = tg * 512
                p.op("dve", (lambda rb=rb, c=c, t0=t0, h=h: (
                    lambda e: e.tensor_scalar_mul(AO[rb:rb + 64, c * S + t0: c * S + t0 + 512],
                                                  vf[rb:rb + 64, t0:t0 + 512],
                                                  wbc[rb:rb + 64, h:h + 1])))(),
                    reads=(bvf, bwbc), writes=(bAO,))
                p.op("dve", (lambda rb=rb, c=c, t0=t0, h=h: (
                    lambda e: e.scalar_tensor_tensor(AO[rb:rb + 64, c * S + t0: c * S + t0 + 512],
                                                     y12[rb:rb + 64, t0:t0 + 512],
                                                     wbc[rb:rb + 64, 16 + h:17 + h],
                                                     AO[rb:rb + 64, c * S + t0: c * S + t0 + 512],
                                                     ALU.mult, ALU.add)))(),
                    reads=(by12, bwbc), writes=(bAO,))
                p.op("dve", (lambda rb=rb, c=c, t0=t0, h=h: (
                    lambda e: e.scalar_tensor_tensor(AO[rb:rb + 64, c * S + t0: c * S + t0 + 512],
                                                     y22[rb:rb + 64, t0:t0 + 512],
                                                     wbc[rb:rb + 64, 32 + h:33 + h],
                                                     AO[rb:rb + 64, c * S + t0: c * S + t0 + 512],
                                                     ALU.mult, ALU.add)))(),
                    reads=(by22, bwbc), writes=(bAO,))
    emit_out_proj_residual(ctx, B, W["Wo"], 3, "AO", transposed=False)


def alloc_l1_attn_a(ctx, B):
    B["qp"] = ctx.sb("qp", [128, S], BF16)
    B["kp"] = ctx.sb("kp", [128, S], BF16)
    B["vf"] = ctx.sb("vf", [128, S], BF16)
    B["vtp"] = ctx.sb("vtp", [128, NT * 128], BF16)
    B["AO"] = ctx.sb("AO", [128, NCH * S], BF16)
    B["PT"] = ctx.sb("PTb", [128, NT * 512], BF16)
    B["Pbuf"] = ctx.sb("Pbuf", [128, S], BF16)
    B["y12"] = ctx.sb("y12", [128, S], BF16)
    B["y22"] = ctx.sb("y22", [128, S], BF16)
    B["y1tok"] = ctx.sb("y1tok", [128, NT * 64], BF16)
    B["wsc"] = ctx.sb("wsc", [1, 128], F32)
    B["wbc"] = ctx.sb("wbc", [128, 48], F32)


def build_l1_attn_a_program():
    spec = (("Wq", [D, D]), ("bq", [D]), ("Wk", [D, D]), ("bk", [D]),
            ("Wv", [D, D]), ("bv", [D]), ("Wo", [D, D]), ("bo", [D]),
            ("w0", [H]), ("w1", [H]), ("wK", [H]))
    return build_common(spec, alloc_l1_attn_a, emit_l1_attn_a, smw=8, smedt=BF16,
                        need_brow="slim")



def emit_l1_attn_b(ctx, B, W):
    p = ctx.p
    X, bX = B["X"]; XN, bXN = B["XN"]
    AO, bAO = B["AO"]
    qh, bqh = B["qh"]; kh, bkh = B["kh"]
    kap, bkap = B["kap"]; qap, bqap = B["qap"]
    xth, bxth = B["xtok_h"]; ef, bef = B["e_feat"]; eth, beth = B["e_tok"]
    t1, bt1 = B["t1"]
    psTb, bpsTb = B["psTbf"]
    identb, bidentb = ctx.load_const("ident_bf16")
    brow, bbrow = B["brow"]
    rowstage, browstage = B["rowstage"]
    scale = 1.0 / np.sqrt(HD)
    p.dma("pool", rowstage[0:1, 0:D], W["b_attn"].rearrange("(a n) -> a n", a=2)[0:1, :],
          writes=(browstage,), stream="bias")
    p.op("dve", (lambda: (lambda e: e.tensor_copy(brow[0:1, 0:D], rowstage[0:1, 0:D])))(),
         reads=(browstage,), writes=(bbrow,))
    p.dma("pool", rowstage[0:1, 0:D], W["b_attn"].rearrange("(a n) -> a n", a=2)[1:2, :],
          writes=(browstage,), stream="bias")
    p.op("dve", (lambda: (lambda e: e.tensor_copy(brow[0:1, D:2 * D], rowstage[0:1, 0:D])))(),
         reads=(browstage,), writes=(bbrow,))
    p.dma("pool", rowstage[0:1, 0:D], W["b_k2"].rearrange("(o n) -> o n", o=1),
          writes=(browstage,), stream="bias")
    p.op("dve", (lambda: (lambda e: e.tensor_copy(brow[0:1, 2 * D:3 * D], rowstage[0:1, 0:D])))(),
         reads=(browstage,), writes=(bbrow,))
    p.dma("pool", rowstage[0:1, 0:D], W["b_proj"].rearrange("(o n) -> o n", o=1),
          writes=(browstage,), stream="bias")
    p.op("dve", (lambda: (
        lambda e: e.tensor_scalar_mul(brow[0:1, 3 * D:4 * D], rowstage[0:1, 0:D], 2.0)))(),
        reads=(browstage,), writes=(bbrow,))

    for c in range(NCH):
        emit_pair_proj(ctx, B, W["w_attn"], (qh, bqh), c, (XN, bXN), transposed=True,
                       bias_slice=brow[0:1, c * 128:(c + 1) * 128])
        emit_pair_proj(ctx, B, W["w_attn"], (kh, bkh), 8 + c, (XN, bXN), transposed=True,
                       bias_slice=brow[0:1, D + c * 128: D + (c + 1) * 128])
        emit_pair_proj(ctx, B, W["w_k2"], (kap, bkap), c, (XN, bXN), transposed=True,
                       bias_slice=brow[0:1, 2 * D + c * 128: 2 * D + (c + 1) * 128],
                       evac=(AF.Sigmoid, scale * 0.02))
        for tg in range(NTG):
            t0 = tg * 512
            p.op("dve", (lambda t0=t0: (
                lambda e: e.tensor_scalar_mul(t1[:, :], qh[:, t0:t0 + 512], float(scale * 0.02))))(),
                reads=(bqh,), writes=(bt1,))
            p.op("dve", (lambda t0=t0: (
                lambda e: e.tensor_scalar_mul(B["sq"][0][:, :], qh[:, t0:t0 + 512], float(scale))))(),
                reads=(bqh,), writes=(B["sq"][1],))
            p.op("dve", (lambda t0=t0: (
                lambda e: e.tensor_tensor(qap[:, t0:t0 + 512], B["sq"][0][:, :], t1[:, :], ALU.min)))(),
                reads=(B["sq"][1], bt1), writes=(bqap,))
        for hh in range(2):
            rb = hh * 64
            for tt in range(NT):
                p.op("pe", (lambda tt=tt, rb=rb, c=c: (
                    lambda e: e.transpose(psTb[:, 0:64],
                                          XN[rb:rb + 64, c * S + tt * 128: c * S + (tt + 1) * 128],
                                          identb[rb:rb + 64, rb:rb + 64])))(),
                    reads=(bXN, bidentb), writes=(bpsTb,))
                p.op("dve", (lambda tt=tt: (
                    lambda e: e.tensor_copy(xth[:, tt * 64:(tt + 1) * 64], psTb[:, 0:64])))(),
                    reads=(bpsTb,), writes=(bxth,))
            qtile_ap = (lambda rb=rb: (lambda tt: qh[rb:rb + 64, tt * 128:(tt + 1) * 128]))()
            vtile = (lambda: (lambda kt: xth[:, kt * 64:(kt + 1) * 64]))()
            def wo_ar(qg, ps, bps, c=c, rb=rb):
                p.op("dve", (lambda qg=qg, c=c, rb=rb, ps=ps: (
                    lambda e: e.tensor_copy(AO[rb:rb + 64, c * S + qg * 512: c * S + qg * 512 + 512],
                                            ps[0:64, :])))(),
                    reads=(bps,), writes=(bAO,))
            emit_attn_core(ctx, B, qtile_ap, kh[rb:rb + 64, :], bkh, vtile, bxth,
                           wo_ar, scale, causal=True, use_softmax=True, bq_src=bqh)
            p.op("dve", (lambda c=c, rb=rb: (
                lambda e: e.tensor_sub(ef[0:64, 0:S - 1],
                                       XN[rb:rb + 64, c * S + 1: c * S + S],
                                       AO[rb:rb + 64, c * S: c * S + S - 1])))(),
                reads=(bXN, bAO), writes=(bef,))
            p.op("dve", (lambda: (lambda e: e.memset(ef[0:64, S - 1:S], 0.0)))(),
                 writes=(bef,))
            for tt in range(NT):
                p.op("pe", (lambda tt=tt: (
                    lambda e: e.transpose(psTb[:, 0:64], ef[0:64, tt * 128:(tt + 1) * 128],
                                          identb[0:64, 0:64])))(),
                    reads=(bef, bidentb), writes=(bpsTb,))
                p.op("dve", (lambda tt=tt: (
                    lambda e: e.tensor_copy(eth[:, tt * 64:(tt + 1) * 64], psTb[:, 0:64])))(),
                    reads=(bpsTb,), writes=(beth,))
            qtile2 = (lambda rb=rb: (lambda tt: qap[rb:rb + 64, tt * 128:(tt + 1) * 128]))()
            vtile2 = (lambda: (lambda kt: eth[:, kt * 64:(kt + 1) * 64]))()
            def wo_ma(qg, ps, bps, c=c, rb=rb):
                p.op("dve", (lambda qg=qg, c=c, rb=rb, ps=ps: (
                    lambda e: e.tensor_add(AO[rb:rb + 64, c * S + qg * 512: c * S + qg * 512 + 512],
                                           AO[rb:rb + 64, c * S + qg * 512: c * S + qg * 512 + 512],
                                           ps[0:64, :])))(),
                    reads=(bps,), writes=(bAO,))
            emit_attn_core(ctx, B, qtile2, kap[rb:rb + 64, :], bkap, vtile2, beth,
                           wo_ma, 1.0, causal=True, use_softmax=False, bq_src=bqap)
    emit_out_proj_residual(ctx, B, W["w_proj"], 3, "AO", transposed=True)


def alloc_l1_attn_b(ctx, B):
    B["qh"] = ctx.sb("qh", [128, S], BF16)
    B["kh"] = ctx.sb("kh", [128, S], BF16)
    B["kap"] = ctx.sb("kap", [128, S], BF16)
    B["qap"] = ctx.sb("qap", [128, S], BF16)
    B["xtok_h"] = ctx.sb("xtok_h", [128, NT * 64], BF16)
    B["e_feat"] = ctx.sb("e_feat", [64, S], BF16)
    B["e_tok"] = B["xtok_h"]
    B["vstage"] = ctx.sb("vstage", [128, 512], BF16)
    B["AO"] = ctx.sb("AO", [128, NCH * S], BF16)
    B["PT"] = ctx.sb("PTb", [128, NT * 512], BF16)
    B["Pbuf"] = ctx.sb("Pbuf", [128, S], BF16)


def build_l1_attn_b_program():
    spec = (("w_attn", [2 * D, D]), ("b_attn", [2 * D]), ("w_k2", [D, D]),
            ("b_k2", [D]), ("w_proj", [D, D]), ("b_proj", [D]))
    return build_common(spec, alloc_l1_attn_b, emit_l1_attn_b, smw=8, smedt=BF16,
                        need_brow="slim")



_SUBLAYERS = (
    ("l0_attn_a", "build_l0_attn_a_program",
     {"Wq": "L0_attn_a_Wq", "bq": "L0_attn_a_bq", "Wk": "L0_attn_a_Wk",
      "bk": "L0_attn_a_bk", "Wv": "L0_attn_a_Wv", "bv": "L0_attn_a_bv",
      "Wo": "L0_attn_a_Wo", "bo": "L0_attn_a_bo"}),
    ("l0_ffn", "build_l0_ffn_program",
     {"router_w": "L0_ffn_router_w", "up_w": "L0_ffn_up_w", "down_w": "L0_ffn_down_w"}),
    ("l0_attn_b", "build_l0_attn_b_program",
     {"Wq": "L0_attn_b_Wq", "Wkv_down": "L0_attn_b_Wkv_down",
      "Wk_up": "L0_attn_b_Wk_up", "Wv_up": "L0_attn_b_Wv_up",
      "Wout": "L0_attn_b_Wout"}),
    ("l0_ffn", None, None),
    ("l1_attn_a", "build_l1_attn_a_program",
     {"Wq": "L1_attn_a_Wq", "bq": "L1_attn_a_bq", "Wk": "L1_attn_a_Wk",
      "bk": "L1_attn_a_bk", "Wv": "L1_attn_a_Wv", "bv": "L1_attn_a_bv",
      "Wo": "L1_attn_a_Wo", "bo": "L1_attn_a_bo", "w0": "L1_attn_a_w0",
      "w1": "L1_attn_a_w1", "wK": "L1_attn_a_wK"}),
    ("l1_ffn", "build_l1_ffn_program",
     {"rms_w": "L1_ffn_rms_w", "gate_w": "L1_ffn_gate_w", "W1": "L1_ffn_W1",
      "b1": "L1_ffn_b1", "W2": "L1_ffn_W2", "b2": "L1_ffn_b2"}),
    ("l1_attn_b", "build_l1_attn_b_program",
     {"w_attn": "L1_attn_b_w_attn", "b_attn": "L1_attn_b_b_attn",
      "w_k2": "L1_attn_b_w_k2", "b_k2": "L1_attn_b_b_k2",
      "w_proj": "L1_attn_b_w_proj", "b_proj": "L1_attn_b_b_proj"}),
    ("l1_ffn", None, None),
)


def kernel(**inputs):
    x = np.ascontiguousarray(inputs["L0_attn_a_x"], dtype=np.float32)
    ncores = 8
    xs = [np.ascontiguousarray(x[2 * c:2 * c + 2]) for c in range(ncores)]
    built = {}
    maps = {}
    for name, bfn, wmap in _SUBLAYERS:
        if bfn is None:
            continue
        nc_prog, consts = globals()[bfn]()
        cmap = {"c_" + k: np.asarray(v) for k, v in consts.items()}
        built[name] = (nc_prog, cmap)
        maps[name] = wmap
    for name, bfn, wmap in _SUBLAYERS:
        nc_prog, cmap = built[name]
        wmap = maps[name]
        wvals = {"w_" + k: np.ascontiguousarray(inputs[v], dtype=np.float32)
                 for k, v in wmap.items()}
        in_maps = []
        for c in range(ncores):
            im = {"x_in": xs[c]}
            im.update(cmap)
            im.update(wvals)
            in_maps.append(im)
        res = run_bass_kernel_spmd(nc_prog, in_maps, core_ids=list(range(ncores)))
        xs = [np.asarray(res.results[c]["x_out"], dtype=np.float32)
              for c in range(ncores)]
    return np.concatenate(xs, axis=0)
